# version 28
# baseline (speedup 1.0000x reference)
import numpy as np
from contextlib import ExitStack
import concourse.bass as bass
import concourse.mybir as mybir
from concourse.bass_utils import run_bass_kernel_spmd

F32 = mybir.dt.float32
BF16 = mybir.dt.bfloat16
U8 = mybir.dt.uint8
AF = mybir.ActivationFunctionType
ALU = mybir.AluOpType
AXX = mybir.AxisListType.X

NCORE = 8
D = 1024
HID = 2048
T_FULL = 16384
C = 2048
HALO = 4
CH = C + HALO
KD = D // 128
KH = HID // 128
MT = 512
NMT = C // MT
BT = 128
NBT = C // BT
MODS = 6 * D // NCORE
EPS = 1e-5

ARENA_COLS = 49152
DBG_NO_CC = False


def build_nc():
    nc = bass.Bass("TRN2", target_bir_lowering=False)
    es = ExitStack()

    dx0 = nc.dram_tensor("x0", [KD, 128, CH], F32, kind="ExternalInput").ap()
    dwin = nc.dram_tensor("w_in_sl", [32, 128, KD * 128], F32, kind="ExternalInput").ap()
    dwg = nc.dram_tensor("w_g_sl", [32, 128, KH * 128], F32, kind="ExternalInput").ap()
    dwo = nc.dram_tensor("w_out_sl", [KD, 128, KH * 128], F32, kind="ExternalInput").ap()
    dwr = nc.dram_tensor("w_r_sl", [KD, 128, KD * 128], F32, kind="ExternalInput").ap()
    dwk = nc.dram_tensor("w_k_sl", [KD, 128, KD * 128], F32, kind="ExternalInput").ap()
    dwv = nc.dram_tensor("w_v_sl", [KD, 128, KD * 128], F32, kind="ExternalInput").ap()
    dwm = nc.dram_tensor("wmod_sl", [KD, 128, MODS], F32, kind="ExternalInput").ap()
    dcvec = nc.dram_tensor("cvec", [128, KD], F32, kind="ExternalInput").ap()
    dconvw = nc.dram_tensor("conv_w4", [4, 128, KH], F32, kind="ExternalInput").ap()
    dconvb = nc.dram_tensor("conv_b", [128, KH], F32, kind="ExternalInput").ap()
    dbgf = nc.dram_tensor("bg_f", [128, KH], F32, kind="ExternalInput").ap()
    dbgi = nc.dram_tensor("bg_i", [128, KH], F32, kind="ExternalInput").ap()
    dfb = nc.dram_tensor("fb", [128, KH], F32, kind="ExternalInput").ap()
    dmur = nc.dram_tensor("mu_r", [128, KD], F32, kind="ExternalInput").ap()
    dmuk = nc.dram_tensor("mu_k", [128, KD], F32, kind="ExternalInput").ap()
    dbmask = nc.dram_tensor("bmask", [128, 1], F32, kind="ExternalInput").ap()
    dcm = nc.dram_tensor("cmask", [7, 128, KH], F32, kind="ExternalInput").ap()
    dcm1 = nc.dram_tensor("cmask1", [7, 128, KH], F32, kind="ExternalInput").ap()
    dy = nc.dram_tensor("y", [KD, 128, C], F32, kind="ExternalOutput").ap()

    g_sp = nc.dram_tensor("g_sp", [NBT, 128, KH * BT], F32).ap()
    c_hi_sp = nc.dram_tensor("c_hi_sp", [KH, 128, C], BF16).ap()
    c_lo_sp = nc.dram_tensor("c_lo_sp", [KH, 128, C], BF16).ap()
    a_sp = nc.dram_tensor("a_sp", [NBT, 128, KH * BT], F32).ap()
    b_sp = nc.dram_tensor("b_sp", [NBT, 128, KH * BT], F32).ap()
    x1_sp = nc.dram_tensor("x1_sp", [KD, 128, C], F32).ap()
    r_sp = nc.dram_tensor("r_sp", [KD, 128, C], F32).ap()
    h2_sp = nc.dram_tensor("h2_sp", [KD, 128, C], F32).ap()
    ccm_in = nc.dram_tensor("ccm_in", [1, MODS], F32)
    ccm_out = nc.dram_tensor("ccm_out", [NCORE, MODS], F32)
    ccc_in = nc.dram_tensor("ccc_in", [128, 2 * KH], F32)
    ccc_out = nc.dram_tensor("ccc_out", [NCORE, 128, 2 * KH], F32)

    AR = es.enter_context(nc.sbuf_tensor("arena", [128, ARENA_COLS], F32)).ap()
    ones1 = es.enter_context(nc.sbuf_tensor("ones1", [1, 128], F32)).ap()
    onescol = es.enter_context(nc.sbuf_tensor("onescol", [128, 1], F32)).ap()
    rowA = es.enter_context(nc.sbuf_tensor("rowA", [1, MT], F32)).ap()
    rowB = es.enter_context(nc.sbuf_tensor("rowB", [1, MT], F32)).ap()
    rowC = es.enter_context(nc.sbuf_tensor("rowC", [1, MT], F32)).ap()
    rowD = es.enter_context(nc.sbuf_tensor("rowD", [1, MT], F32)).ap()
    rowM = es.enter_context(nc.sbuf_tensor("rowM", [1, MODS], F32)).ap()
    eps1 = es.enter_context(nc.sbuf_tensor("eps1", [1, 1], F32)).ap()
    selm = es.enter_context(nc.sbuf_tensor("selm", [128, KD], U8)).ap()

    pm = [es.enter_context(nc.psum_tensor(f"pm{i}", [128, MT], F32)).ap()
          for i in range(4)]
    pH = es.enter_context(nc.psum_tensor("pH", [128, 16], F32)).ap()
    pS = es.enter_context(nc.psum_tensor("pS", [1, MT], F32)).ap()
    pQ = es.enter_context(nc.psum_tensor("pQ", [1, MT], F32)).ap()
    pR = es.enter_context(nc.psum_tensor("pR", [128, MT], F32)).ap()

    sems = {}
    for k in ("T", "V", "S", "GC", "W0", "W1", "WB", "WA0", "WA1", "P", "PA0",
              "PA1", "PB0", "PB1", "PH0", "PH1", "PX0", "PX1", "G", "G0", "G1",
              "GO", "CC"):
        sems[k] = es.enter_context(nc.semaphore(f"sem{k}"))

    Q = {k: [] for k in ("T", "V", "S", "Y", "G")}
    DMAS = ("W0", "W1", "WB", "WA0", "WA1", "P", "PA0", "PA1", "PB0", "PB1",
            "PH0", "PH1", "PX0", "PX1", "G", "G0", "G1", "GO", "CC")
    cnt = {k: 0 for k in ("T", "V", "S", "GC") + DMAS}
    maxwait = {k: 0 for k in cnt}

    def q(stream, fn):
        Q[stream].append(fn)

    def qw(stream, semk, val=None):
        if semk in DMAS:
            val = None
        v = cnt[semk] if val is None else val
        assert v <= cnt[semk]
        maxwait[semk] = max(maxwait[semk], v)
        tgt = sems[semk]
        q(stream, lambda e, _t=tgt, _v=v: e.wait_ge(_t, _v))

    def qi(stream, semk, n, fn):
        if stream == "V":
            qw("V", "V")
        elif stream == "S":
            qw("S", "S")
        elif stream == "G" and semk == "GC":
            qw("G", "GC")
        cnt[semk] += n
        tgt = sems[semk]
        q(stream, lambda e, _f=fn, _t=tgt, _n=n: _f(e).then_inc(_t, _n))
        return cnt[semk]

    def qn(stream, fn):
        q(stream, fn)

    def qtg(fns):
        for f in fns[:-1]:
            qn("T", f)
        return qi("T", "T", 1, fns[-1])

    def qbar():
        own = {"T": ("T",), "V": ("V",),
               "S": ("S", "P", "PA0", "PA1", "PB0", "PB1", "PH0", "PH1",
                     "PX0", "PX1"),
               "Y": ("W0", "W1", "WB", "WA0", "WA1"),
               "G": ("GC", "G", "G0", "G1", "GO", "CC")}
        for stream in ("T", "V", "S", "Y", "G"):
            mine = own[stream]
            for semk in ("T", "V", "S", "GC", "W0", "W1", "WB", "WA0", "WA1", "P",
                         "PA0", "PA1", "PB0", "PB1", "PH0", "PH1", "PX0", "PX1",
                         "G", "G0", "G1", "GO"):
                if semk in mine or cnt[semk] == 0:
                    continue
                qw(stream, semk)

    class Alloc:
        def __init__(self):
            self.off = 0
            self.peak = 0
        def take(self, n):
            a = self.off
            self.off += n
            self.peak = max(self.peak, self.off)
            assert self.off <= ARENA_COLS, f"arena overflow {self.off}"
            return AR[:, a:a + n]
        def mark(self):
            return self.off
        def reset(self, m):
            self.off = m

    al = Alloc()

    def takeH(n):
        return al.take((n + 1) // 2).bitcast(BF16)

    runA = al.take(KH)
    runB = al.take(KH)
    Lsum = al.take(KH)
    accb = al.take(4)
    gate_b = al.take(KH)
    carA = al.take(KH)
    carB = al.take(KH)
    ab_sb = al.take(2 * KH)
    aball = al.take(NCORE * 2 * KH)
    svec = al.take(KD)
    sa_m = al.take(KD); ca1_m = al.take(KD); ga_m = al.take(KD)
    sb_m = al.take(KD); cb1_m = al.take(KD); gb_m = al.take(KD)
    rawca = al.take(KD); rawcb = al.take(KD)
    cvec_t = al.take(KD)
    convw_t = [al.take(KH) for _ in range(4)]
    convb_t = al.take(KH)
    bgf_t = al.take(KH); bgi_t = al.take(KH); fb_t = al.take(KH)
    kvec = al.take(KH)
    mur_t = al.take(KD); muk_t = al.take(KD)
    bmask_t = al.take(1)
    cm_t = al.take(7 * KH); cm1_t = al.take(7 * KH)
    x0b = al.take(KD)
    x1b = al.take(KD)
    mb = al.take(KH)
    h2bcol = al.take(KD)
    tmp16a = al.take(KH); tmp16b = al.take(KH); tmp16c = al.take(KH)
    selsrcA = al.take(KD); selsrcB = al.take(KD)
    PERSIST = al.mark()

    def dma(stream, semk, out_ap, in_ap, slow=False):
        if slow:
            return qi(stream, semk, 16,
                      lambda e, _o=out_ap, _i=in_ap: e.dma_start(
                          out=_o, in_=_i, allow_slow_non_contiguous=True))
        return qi(stream, semk, 16,
                  lambda e, _o=out_ap, _i=in_ap: e.dma_start(out=_o, in_=_i))

    lnst = {"psq_user": 0}

    def ln_stats(xbuf, L, kt_n, repR, repQ, xsqp, xv=0):
        nch = kt_n * 128
        qw("T", "V", max(lnst["psq_user"], xv))
        qtg([(lambda e, _r=xbuf[:, kt * L:(kt + 1) * L], _st=(kt == 0),
              _sp=(kt == kt_n - 1), _L=L:
              e.matmul(pS[:, :_L], onescol, _r, start=_st, stop=_sp))
             for kt in range(kt_n)])
        mmdone = []
        for kt in range(kt_n):
            if kt >= 2:
                qw("S", "T", mmdone[kt - 2])
            sqd = qi("S", "S", 1, lambda e, _x=xbuf[:, kt * L:(kt + 1) * L],
                     _q=xsqp[kt % 2], _L=L: e.activation(_q[:, :_L], _x, AF.Square))
            qw("T", "S", sqd)
            mmdone.append(qi("T", "T", 1, lambda e, _q=xsqp[kt % 2], _L=L,
                             _st=(kt == 0), _sp=(kt == kt_n - 1):
                             e.matmul(pQ[:, :_L], onescol, _q[:, :_L],
                                      start=_st, stop=_sp)))
        qw("V", "T")
        inv = 1.0 / nch
        qi("V", "V", 1, lambda e, _L=L: e.tensor_single_scalar(
            rowA[:, :_L], pS[:, :_L], inv, op=ALU.mult))
        lnst["psq_user"] = qi("V", "V", 1, lambda e, _L=L: e.tensor_single_scalar(
            rowB[:, :_L], pQ[:, :_L], inv, op=ALU.mult))
        qw("S", "V")
        qi("S", "S", 1, lambda e, _L=L: e.activation(rowC[:, :_L], rowA[:, :_L], AF.Square))
        qw("V", "S")
        qi("V", "V", 1, lambda e, _L=L: e.tensor_sub(rowD[:, :_L], rowB[:, :_L], rowC[:, :_L]))
        qw("S", "V")
        qi("S", "S", 1, lambda e, _L=L: e.activation(rowC[:, :_L], rowD[:, :_L], AF.Sqrt, bias=eps1))
        qw("V", "S")
        qi("V", "V", 1, lambda e, _L=L: e.reciprocal(rowB[:, :_L], rowC[:, :_L]))
        qi("V", "V", 1, lambda e, _L=L: e.tensor_mul(rowC[:, :_L], rowA[:, :_L], rowB[:, :_L]))
        rows_done = qi("V", "V", 1, lambda e, _L=L: e.tensor_single_scalar(
            rowD[:, :_L], rowC[:, :_L], -1.0, op=ALU.mult))
        qw("T", "V", rows_done)
        qtg([lambda e, _L=L: e.matmul(pR[:, :_L], ones1, rowB[:, :_L], start=True, stop=True)])
        qw("V", "T")
        rcp = qi("V", "V", 1, lambda e, _L=L, _r=repR: e.tensor_copy(_r[:, :_L], pR[:, :_L]))
        qw("T", "V", rcp)
        qtg([lambda e, _L=L: e.matmul(pR[:, :_L], ones1, rowD[:, :_L], start=True, stop=True)])
        qw("V", "T")
        qi("V", "V", 1, lambda e, _L=L, _r=repQ: e.tensor_copy(_r[:, :_L], pR[:, :_L]))

    dma("Y", "W0", cvec_t, dcvec)
    for k in range(4):
        dma("Y", "W0", convw_t[k], dconvw[k])
    dma("Y", "W0", convb_t, dconvb)
    dma("Y", "W0", bgf_t, dbgf)
    dma("Y", "W0", bgi_t, dbgi)
    dma("Y", "W0", fb_t, dfb)
    dma("Y", "W0", mur_t, dmur)
    dma("Y", "W0", muk_t, dmuk)
    dma("Y", "W0", bmask_t, dbmask)
    for r in range(7):
        dma("Y", "W0", cm_t[:, r * KH:(r + 1) * KH], dcm[r])
        dma("Y", "W0", cm1_t[:, r * KH:(r + 1) * KH], dcm1[r])

    m0mark = al.mark()
    wm_sl = [al.take(MODS) for _ in range(KD)]
    for kt in range(KD):
        dma("Y", "W1", wm_sl[kt], dwm[kt])

    qi("V", "V", 1, lambda e: e.memset(ones1, 1.0))
    qi("V", "V", 1, lambda e: e.memset(eps1, EPS))
    qi("V", "V", 1, lambda e: e.memset(onescol, 1.0))
    qi("V", "V", 1, lambda e: e.memset(runA, 0.0))

    qw("S", "W0")
    qw("S", "V")
    qi("S", "S", 1, lambda e: e.activation(svec, cvec_t, AF.Silu))
    qi("S", "S", 1, lambda e: e.activation(tmp16a, fb_t, AF.Exp))
    qi("S", "S", 1, lambda e: e.activation(tmp16b, tmp16a, AF.Ln, bias=onescol))
    qw("V", "S")
    qi("V", "V", 1, lambda e: e.tensor_single_scalar(kvec, tmp16b, -8.0, op=ALU.mult))

    qw("T", "W1")
    qw("T", "S")
    n512 = MODS // MT
    segs = [(i * MT, MT) for i in range(n512)]
    if MODS - n512 * MT:
        segs.append((n512 * MT, MODS - n512 * MT))
    for si, (o, L) in enumerate(segs):
        ps = pS if si == 0 else pQ
        qtg([(lambda e, _p=ps, _kt=kt, _o=o, _L=L:
              e.matmul(_p[:, :_L], svec[:, _kt:_kt + 1], wm_sl[_kt][:, _o:_o + _L],
                       start=(_kt == 0), stop=(_kt == KD - 1)))
             for kt in range(KD)])
    qw("V", "T")
    for si, (o, L) in enumerate(segs):
        ps = pS if si == 0 else pQ
        qi("V", "V", 1, lambda e, _p=ps, _o=o, _L=L:
           e.tensor_copy(rowM[:, _o:_o + _L], _p[:, :_L]))
    lnst["psq_user"] = cnt["V"]
    qw("G", "V")
    dma("G", "G", ccm_in.ap(), rowM)
    qw("G", "G")
    if DBG_NO_CC:
        for _r in range(NCORE):
            dma("G", "G", ccm_out.ap()[_r:_r + 1, :], ccm_in.ap())
        qw("G", "G")
    else:
        qi("G", "CC", 1, lambda e: e.collective_compute(
            "AllGather", ALU.bypass, replica_groups=[list(range(NCORE))],
            ins=[ccm_in.ap().opt()], outs=[ccm_out.ap().opt()]))
        qw("G", "CC")
    modflat = ccm_out.ap().rearrange("a b -> (a b)")
    for v, dst in [(0, sa_m), (1, rawca), (2, ga_m), (3, sb_m), (4, rawcb), (5, gb_m)]:
        src = modflat[v * D:(v + 1) * D].rearrange("(kt p) -> p kt", p=128)
        dma("G", "G", dst, src, slow=True)
    dma("G", "G", x0b, dx0.transpose([1, 0, 2])[:, :, 3:4].rearrange("p a b -> p (a b)"),
        slow=True)

    al.reset(m0mark)

    h1hi = takeH(KD * CH)
    h1lo = takeH(KD * CH)
    a0mark = al.mark()
    h1f = al.take(KD * MT)
    x0bf = [al.take(KD * MT) for _ in range(2)]
    repR_t = [al.take(MT) for _ in range(NMT + 1)]
    repQ_t = [al.take(MT) for _ in range(NMT + 1)]
    uscr0 = al.take(MT)
    uscr0b = al.take(MT)
    xsq0 = [al.take(MT) for _ in range(2)]
    tiles_a0 = [(HALO, MT), (0, HALO)] + [(HALO + m * MT, MT)
                                          for m in range(1, NMT)]
    TIX = [0, 2, 3, 4]
    h1doneV = {}
    h1doneS = {}
    x0_userV = [0, 0]
    x0_userS = [0, 0]
    x0_userT = [0, 0]
    h1hi3 = h1hi.rearrange("p (a b) -> p a b", a=KD)
    h1lo3 = h1lo.rearrange("p (a b) -> p a b", a=KD)
    for ti, (off, L) in enumerate(tiles_a0):
        xb = x0bf[ti % 2]
        qw("Y", "V", x0_userV[ti % 2])
        qw("Y", "S", x0_userS[ti % 2])
        qw("Y", "T", x0_userT[ti % 2])
        dma("Y", f"WA{ti % 2}", xb[:, :KD * L].rearrange("p (a b) -> p a b", a=KD),
            dx0.transpose([1, 0, 2])[:, :, off:off + L])
        qw("V", f"WA{ti % 2}")
        qw("S", f"WA{ti % 2}")
        qw("T", f"WA{ti % 2}")
        ln_stats(xb[:, :KD * L], L, KD, repR_t[ti], repQ_t[ti], xsq0)
        x0_userS[ti % 2] = cnt["S"]
        x0_userT[ti % 2] = cnt["T"]
        x0_userV[ti % 2] = cnt["V"]

    qw("V", "G")
    qi("V", "V", 1, lambda e: e.tensor_single_scalar(ca1_m, rawca, 1.0, op=ALU.add))
    qi("V", "V", 1, lambda e: e.tensor_single_scalar(cb1_m, rawcb, 1.0, op=ALU.add))
    for g in (ga_m, gb_m):
        qi("V", "V", 1, lambda e, _g=g: e.tensor_single_scalar(selm, _g, 0.0, op=ALU.is_ge))
        qi("V", "V", 1, lambda e, _g=g: e.tensor_single_scalar(selsrcA, _g, 0.25, op=ALU.max))
        qi("V", "V", 1, lambda e, _g=g: e.tensor_single_scalar(selsrcB, _g, -0.25, op=ALU.min))
        qi("V", "V", 1, lambda e, _g=g: e.select(_g, selm, selsrcA, selsrcB, add_drain=True))

    a1mark = al.mark()
    stagW = [al.take(KD * 128) for _ in range(2)]
    slabWhi = [takeH(KD * 128) for _ in range(2)]
    slabWlo = [takeH(KD * 128) for _ in range(2)]
    xcb = al.take(HALO + C)
    cA = h1f[:, 0:C]
    cB = al.take(C)
    chis = [h1f[:, C + i * (C // 2): C + (i + 1) * (C // 2)].bitcast(BF16)
            for i in range(2)]
    clos = [takeH(C), takeH(C)]
    gbufs = [al.take(C), cB]
    htmp = al.take(HALO)
    slabW_userT = [0, 0]
    stag_userS = [0, 0]
    stag_userV = [0, 0]
    split_S = [0, 0]
    split_V = [0, 0]
    pm_freeV = [0, 0, 0, 0]
    pm_freeS = [0, 0, 0, 0]
    pH_freeV = 0
    pH_freeS = 0

    def mm3(ps, kts, whi, wlo, rhi_fn, rlo_fn, first=True, stop=True):
        trip = []
        for kt in kts:
            trip.append((whi[:, kt * 128:(kt + 1) * 128], rhi_fn(kt)))
            trip.append((whi[:, kt * 128:(kt + 1) * 128], rlo_fn(kt)))
            trip.append((wlo[:, kt * 128:(kt + 1) * 128], rhi_fn(kt)))
        last = len(trip) - 1
        return qtg([(lambda e, _w=w, _r=r, _st=(first and i == 0),
                     _sp=(stop and i == last), _p=ps:
                     e.matmul(_p, _w, _r, start=_st, stop=_sp))
                    for i, (w, r) in enumerate(trip)])

    oc_order = list(range(16, 32)) + list(range(16))
    slabW_userT = [0, 0]
    dma("Y", "W0", stagW[0], dwin[oc_order[0]])
    dma("Y", "W1", stagW[1], dwin[oc_order[1]])
    qw("S", "W0")
    split_S[0] = stag_userS[0] = qi(
        "S", "S", 1, lambda e: e.activation(slabWhi[0], stagW[0], AF.Copy))
    qw("V", "S", split_S[0])
    split_V[0] = stag_userV[0] = qi(
        "V", "V", 1, lambda e: e.tensor_sub(slabWlo[0], stagW[0], slabWhi[0]))
    cB_freeV = 0

    for ti, (off, L) in enumerate(tiles_a0):
        xb = x0bf[ti % 2]
        qw("Y", "V", x0_userV[ti % 2])
        qw("Y", "S", x0_userS[ti % 2])
        qw("Y", "T", x0_userT[ti % 2])
        dma("Y", f"WA{ti % 2}", xb[:, :KD * L].rearrange("p (a b) -> p a b", a=KD),
            dx0.transpose([1, 0, 2])[:, :, off:off + L])
        qw("V", f"WA{ti % 2}")
        if ti > 0:
            qw("V", "S", h1doneS[ti - 1])
            qw("V", "GC", h1doneV[ti - 1])
        for kt in range(KD):
            qi("V", "V", 1, lambda e, _x=xb[:, kt * L:(kt + 1) * L], _l=L, _t=ti:
               e.tensor_mul(uscr0[:, :_l], _x, repR_t[_t][:, :_l]))
            qi("V", "V", 1, lambda e, _l=L, _t=ti:
               e.tensor_add(uscr0b[:, :_l], uscr0[:, :_l], repQ_t[_t][:, :_l]))
            qi("V", "V", 1, lambda e, _h=h1f[:, kt * L:(kt + 1) * L],
               _k=kt, _l=L: e.tensor_scalar(
                   _h, uscr0b[:, :_l], ca1_m[:, _k:_k + 1], sa_m[:, _k:_k + 1],
                   op0=ALU.mult, op1=ALU.add))
        x0_userV[ti % 2] = cnt["V"]
        hiv = h1hi3[:, :, off:off + L]
        lov = h1lo3[:, :, off:off + L]
        h1fv = h1f[:, :KD * L].rearrange("p (a b) -> p a b", a=KD)
        qw("S", "V")
        h1doneS[ti] = qi("S", "S", 1, lambda e, _o=hiv, _i=h1fv:
                         e.activation(_o, _i, AF.Copy))
        qw("G", "S", h1doneS[ti])
        h1doneV[ti] = qi("G", "GC", 1, lambda e, _o=lov, _i=h1fv, _h=hiv:
                         e.tensor_sub(_o, _i, _h))

    for oci, oc in enumerate(oc_order):
        buf = oci % 2
        nbuf = buf ^ 1
        if oci + 1 < 32:
            qw("S", f"W{nbuf}")
            qw("S", "T", slabW_userT[nbuf])
            split_S[nbuf] = stag_userS[nbuf] = qi(
                "S", "S", 1, lambda e, _b=nbuf: e.activation(
                    slabWhi[_b], stagW[_b], AF.Copy))
            qw("V", "S", split_S[nbuf])
            qw("V", "T", slabW_userT[nbuf])
            split_V[nbuf] = stag_userV[nbuf] = qi(
                "V", "V", 1, lambda e, _b=nbuf: e.tensor_sub(
                    slabWlo[_b], stagW[_b], slabWhi[_b]))
        if oci + 2 < 32:
            qw("Y", "S", stag_userS[buf])
            qw("Y", "V", stag_userV[buf])
            dma("Y", f"W{buf}", stagW[buf], dwin[oc_order[oci + 2]])
        if oc < 16:
            qw("S", f"PH{oc % 2}")
            if oc == 1:
                qw("S", "V", cB_freeV)
        for m in range(NMT):
            qw("T", "V", max(pm_freeV[m], split_V[buf]))
            qw("T", "GC", h1doneV[TIX[m]])
            qw("T", "S", max(pm_freeS[m], h1doneS[TIX[m]], split_S[buf]))
            mm3(pm[m], range(KD), slabWhi[buf], slabWlo[buf],
                lambda kt, _m=m: h1hi3[:, kt, HALO + _m * MT: HALO + (_m + 1) * MT],
                lambda kt, _m=m: h1lo3[:, kt, HALO + _m * MT: HALO + (_m + 1) * MT])
            tdone = cnt["T"]
            if oc >= 16:
                qw("V", "T", tdone)
                pm_freeV[m] = qi("V", "V", 1, lambda e, _m=m, _o=HALO + m * MT:
                                 e.tensor_copy(xcb[:, _o:_o + MT], pm[_m]))
            else:
                qw("S", "T", tdone)
                pm_freeS[m] = qi("S", "S", 1, lambda e, _m=m, _o=m * MT,
                                 _g=gbufs[oc % 2]:
                                 e.activation(_g[:, _o:_o + MT], pm[_m], AF.Gelu))
        qw("T", "V", pH_freeV)
        qw("T", "GC", h1doneV[1])
        qw("T", "S", max(pH_freeS, h1doneS[1]))
        slabW_userT[buf] = mm3(pH[:, :HALO], range(KD), slabWhi[buf], slabWlo[buf],
                               lambda kt: h1hi3[:, kt, 0:HALO],
                               lambda kt: h1lo3[:, kt, 0:HALO])
        tdone = cnt["T"]
        if oc >= 16:
            b = oc - 16
            qw("V", "T", tdone)
            qi("V", "V", 1, lambda e: e.tensor_copy(htmp, pH[:, :HALO]))
            pH_freeV = qi("V", "V", 1, lambda e: e.tensor_scalar(
                xcb[:, 0:HALO], htmp, bmask_t, None, op0=ALU.mult))
            qw("V", f"PX{b % 2}")
            qw("V", "S", h1doneS[NMT])
            qw("V", "GC", h1doneV[NMT])
            qi("V", "V", 1, lambda e, _b=b: e.tensor_scalar(
                cA, xcb[:, 4:4 + C], convw_t[3][:, _b:_b + 1], convb_t[:, _b:_b + 1],
                op0=ALU.mult, op1=ALU.add))
            qi("V", "V", 1, lambda e, _b=b: e.scalar_tensor_tensor(
                cB, xcb[:, 3:3 + C], convw_t[2][:, _b:_b + 1], cA, op0=ALU.mult, op1=ALU.add))
            qi("V", "V", 1, lambda e, _b=b: e.scalar_tensor_tensor(
                cA, xcb[:, 2:2 + C], convw_t[1][:, _b:_b + 1], cB, op0=ALU.mult, op1=ALU.add))
            vdone = qi("V", "V", 1, lambda e, _b=b: e.scalar_tensor_tensor(
                cB, xcb[:, 1:1 + C], convw_t[0][:, _b:_b + 1], cA, op0=ALU.mult, op1=ALU.add))
            qw("S", "V", vdone)
            qw("S", f"PX{b % 2}")
            hs = qi("S", "S", 1, lambda e, _c=chis[b % 2]:
                    e.activation(_c, cB, AF.Copy))
            qw("V", "S", hs)
            ls = qi("V", "V", 1, lambda e, _c=chis[b % 2], _l=clos[b % 2]:
                    e.tensor_sub(_l, cB, _c))
            if b == 15:
                cB_freeV = ls
            dma("S", f"PX{b % 2}", c_hi_sp[b], chis[b % 2])
            qw("S", "V", ls)
            dma("S", f"PX{b % 2}", c_lo_sp[b], clos[b % 2])
        else:
            qw("S", "T", tdone)
            pH_freeS = qi("S", "S", 1, lambda e, _oc=oc: e.activation(
                gate_b[:, _oc:_oc + 1], pH[:, 3:4], AF.Gelu))
            dma("S", f"PH{oc % 2}",
                g_sp[:, :, oc * BT:(oc + 1) * BT].transpose([1, 0, 2]),
                gbufs[oc % 2].rearrange("p (a b) -> p a b", a=NBT))
    al.reset(a0mark)

    qbar()
    al.reset(PERSIST)
    convhi = takeH(KH * C)
    convlo = takeH(KH * C)
    a3mark = al.mark()
    stagG = al.take(KH * 128)
    sFhi = [takeH(KH * 128), takeH(KH * 128)]
    sFlo = [takeH(KH * 128), takeH(KH * 128)]
    slabIhi = takeH(KH * 128)
    slabIlo = takeH(KH * 128)
    sgf = al.take(MT)
    Lbuf = al.take(MT)
    albuf = [al.take(MT) for _ in range(2)]
    tbufA = al.take(MT)
    tbufB = al.take(MT)
    sibuf = al.take(MT)
    btbuf = [al.take(MT) for _ in range(2)]
    hscr = al.take(MT)
    convrec = al.take(MT)
    qw("G", "P")
    for b in range(KH):
        sem = ("G0", "G1", "GO", "G")[b // 4]
        dma("G", sem, convhi[:, b * C:(b + 1) * C], c_hi_sp[b])
        dma("G", sem, convlo[:, b * C:(b + 1) * C], c_lo_sp[b])
    slab_userT = 0
    pm_freeV = [0, 0, 0, 0]
    pm_freeS = [0, 0, 0, 0]
    fF_userT = [0, 0]
    fI_userT = 0
    fs = [0, 0]
    fv = [0, 0]
    is_ = iv = 0

    def a3_split(hi, lo, userT, wq):
        qw("S", wq)
        qw("S", "T", userT)
        s = qi("S", "S", 1, lambda e, _h=hi: e.activation(_h, stagG, AF.Copy))
        qw("V", "S", s)
        qw("V", "T", userT)
        v = qi("V", "V", 1, lambda e, _h=hi, _l=lo: e.tensor_sub(_l, stagG, _h))
        return s, v

    dma("Y", "W0", stagG, dwg[0])
    fs[0], fv[0] = a3_split(sFhi[0], sFlo[0], 0, "W0")
    qw("Y", "S", fs[0])
    qw("Y", "V", fv[0])
    dma("Y", "W1", stagG, dwg[16])
    for oc in range(16):
        p = oc % 2
        pn = p ^ 1
        is_, iv = a3_split(slabIhi, slabIlo, fI_userT, "W1")
        if oc + 1 < 16:
            qw("Y", "S", is_)
            qw("Y", "V", iv)
            dma("Y", "W0", stagG, dwg[oc + 1])
            fs[pn], fv[pn] = a3_split(sFhi[pn], sFlo[pn], fF_userT[pn], "W0")
            qw("Y", "S", fs[pn])
            qw("Y", "V", fv[pn])
            dma("Y", "W1", stagG, dwg[16 + oc + 1])
        for m in range(NMT):
            fidx = 2 * (m % 2)
            iidx = 2 * (m % 2) + 1
            qw("T", "V", max(pm_freeV[fidx], fv[p]))
            qw("T", "S", max(pm_freeS[fidx], fs[p]))
            for qtr in range(4):
                qw("T", ("G0", "G1", "GO", "G")[qtr])
                mm3(pm[fidx], range(qtr * 4, (qtr + 1) * 4), sFhi[p], sFlo[p],
                    lambda kt, _m=m: convhi[:, kt * C + _m * MT: kt * C + (_m + 1) * MT],
                    lambda kt, _m=m: convlo[:, kt * C + _m * MT: kt * C + (_m + 1) * MT],
                    first=(qtr == 0), stop=(qtr == 3))
            tf = cnt["T"]
            if m == NMT - 1:
                fF_userT[p] = tf
            qw("T", "V", max(pm_freeV[iidx], iv))
            qw("T", "S", max(pm_freeS[iidx], is_))
            slab_userT = mm3(
                pm[iidx], range(KH), slabIhi, slabIlo,
                lambda kt, _m=m: convhi[:, kt * C + _m * MT: kt * C + (_m + 1) * MT],
                lambda kt, _m=m: convlo[:, kt * C + _m * MT: kt * C + (_m + 1) * MT])
            ti_ = cnt["T"]
            if m == NMT - 1:
                fI_userT = ti_
            ab = albuf[m % 2]
            bb = btbuf[m % 2]
            qw("S", "V")
            qw("S", "T", tf)
            qi("S", "S", 1, lambda e, _f=fidx, _oc=oc: e.activation(
                sgf, pm[_f], AF.Sigmoid, bias=bgf_t[:, _oc:_oc + 1]))
            pm_freeS[fidx] = cnt["S"]
            qw("V", "S")
            qi("V", "V", 1, lambda e, _oc=oc, _m=m: e.tensor_scalar(
                Lbuf, sgf, kvec[:, _oc:_oc + 1], None, op0=ALU.mult, op1=ALU.add,
                accum_out=accb[:, _m:_m + 1]))
            qw("S", "V")
            qw("S", f"PA{m % 2}")
            qi("S", "S", 1, lambda e, _a=ab: e.activation(_a, Lbuf, AF.Exp))
            dma("S", f"PA{m % 2}",
                a_sp[NMT * m:NMT * (m + 1), :, oc * BT:(oc + 1) * BT].transpose([1, 0, 2]),
                ab.rearrange("p (a b) -> p a b", a=NMT))
            qi("S", "S", 1, lambda e, _a=ab: e.activation(tbufA, _a, AF.Square))
            qw("V", "S")
            qi("V", "V", 1, lambda e: e.tensor_scalar(
                tbufB, tbufA, -1.0, 1.0 + 1e-6, op0=ALU.mult, op1=ALU.add))
            qw("S", "V")
            qi("S", "S", 1, lambda e: e.activation(tbufA, tbufB, AF.Sqrt))
            qw("S", "T", ti_)
            qi("S", "S", 1, lambda e, _i=iidx, _oc=oc: e.activation(
                sibuf, pm[_i], AF.Sigmoid, bias=bgi_t[:, _oc:_oc + 1]))
            pm_freeS[iidx] = cnt["S"]
            qi("V", "V", 1, lambda e, _oc=oc, _m=m: e.tensor_add(
                convrec, convhi[:, _oc * C + _m * MT: _oc * C + (_m + 1) * MT],
                convlo[:, _oc * C + _m * MT: _oc * C + (_m + 1) * MT]))
            qw("V", "S")
            qw("V", f"PB{m % 2}")
            qi("V", "V", 1, lambda e: e.tensor_mul(tbufB, tbufA, sibuf))
            vdone = qi("V", "V", 1, lambda e, _b=bb: e.tensor_mul(_b, tbufB, convrec))
            qw("S", "V", vdone)
            dma("S", f"PB{m % 2}",
                b_sp[NMT * m:NMT * (m + 1), :, oc * BT:(oc + 1) * BT].transpose([1, 0, 2]),
                bb.rearrange("p (a b) -> p a b", a=NMT))
            qi("V", "V", 1, lambda e, _a=ab, _b=bb, _oc=oc: e.tensor_tensor_scan(
                hscr, _a, _b, runA[:, _oc:_oc + 1], op0=ALU.mult, op1=ALU.add))
            pm_freeV[fidx] = pm_freeV[iidx] = qi(
                "V", "V", 1, lambda e, _oc=oc: e.tensor_copy(
                    runA[:, _oc:_oc + 1], hscr[:, MT - 1:MT]))
        qi("V", "V", 1, lambda e, _oc=oc: e.reduce_sum(
            Lsum[:, _oc:_oc + 1], accb, axis=AXX))
    al.reset(a3mark)

    qbar()
    qw("S", "V")
    qi("S", "S", 1, lambda e: e.activation(ab_sb[:, 0:KH], Lsum, AF.Exp))
    qw("V", "S")
    qi("V", "V", 1, lambda e: e.tensor_copy(ab_sb[:, KH:2 * KH], runA))
    qw("G", "V")
    dma("G", "G", ccc_in.ap(), ab_sb)
    qw("G", "G")
    if DBG_NO_CC:
        for _r in range(NCORE):
            dma("G", "G", ccc_out.ap()[_r], ccc_in.ap())
        qw("G", "G")
    else:
        qi("G", "CC", 1, lambda e: e.collective_compute(
            "AllGather", ALU.bypass, replica_groups=[list(range(NCORE))],
            ins=[ccc_in.ap().opt()], outs=[ccc_out.ap().opt()]))
        qw("G", "CC")
    dma("G", "G", aball.rearrange("p (a b) -> p a b", a=NCORE),
        ccc_out.ap().transpose([1, 0, 2]))
    qw("V", "G")
    qi("V", "V", 1, lambda e: e.memset(carA, 0.0))
    aview = aball.rearrange("p (a b) -> p a b", a=NCORE)[:, 0:7, 0:KH]
    bview = aball.rearrange("p (a b) -> p a b", a=NCORE)[:, 0:7, KH:2 * KH]
    am_raw = al.take(7 * KH)
    am_all = al.take(7 * KH)
    bm_all = al.take(7 * KH)
    qi("V", "V", 1, lambda e: e.tensor_tensor(
        am_raw.rearrange("p (a b) -> p a b", a=7), aview,
        cm_t.rearrange("p (a b) -> p a b", a=7), op=ALU.mult))
    qi("V", "V", 1, lambda e: e.tensor_add(am_all, am_raw, cm1_t))
    qi("V", "V", 1, lambda e: e.tensor_tensor(
        bm_all.rearrange("p (a b) -> p a b", a=7), bview,
        cm_t.rearrange("p (a b) -> p a b", a=7), op=ALU.mult))
    cur, nxt = carA, carB
    for r in range(NCORE - 1):
        qi("V", "V", 1, lambda e, _c=cur, _r=r: e.tensor_mul(
            tmp16c, _c, am_all[:, _r * KH:(_r + 1) * KH]))
        qi("V", "V", 1, lambda e, _n=nxt, _r=r: e.tensor_add(
            _n, tmp16c, bm_all[:, _r * KH:(_r + 1) * KH]))
        cur, nxt = nxt, cur
    carry = cur
    qi("V", "V", 1, lambda e, _c=carry: e.tensor_copy(runB, _c))
    qi("V", "V", 1, lambda e, _c=carry: e.tensor_mul(mb, gate_b, _c))

    qbar()
    al.reset(PERSIST)
    slot_hi = [takeH(KH * MT) for _ in range(2)]
    slot_lo = [takeH(KH * MT) for _ in range(2)]
    slotf = al.take(KH * BT)
    at_t = [al.take(KH * BT) for _ in range(2)]
    bt_t = [al.take(KH * BT) for _ in range(2)]
    gt_t = al.take(KH * BT)
    h_t = al.take(KH * BT)
    stagO = al.take(KH * 128)
    sOhi = [takeH(KH * 128) for _ in range(2)]
    sOlo = [takeH(KH * 128) for _ in range(2)]
    xop = [al.take(MT) for _ in range(2)]
    x1loc = al.take(KD * MT)
    h2piece = [al.take(MT) for _ in range(2)]
    repR3 = al.take(MT)
    repQ3 = al.take(MT)
    uscr3 = al.take(MT)
    uscr3b = al.take(MT)
    xsq3 = [al.take(MT) for _ in range(2)]

    ab_scan_user = [0, 0]
    g_user = [0]
    slotf_hiS = [0]

    def b1_prefetch_ab(bt):
        if bt >= NBT:
            return
        p = bt % 2
        qw("Y", "V", ab_scan_user[p])
        dma("Y", f"WA{p}", at_t[p], a_sp[bt])
        dma("Y", f"WA{p}", bt_t[p], b_sp[bt])

    def b1_tile(m, j, tfree):
        bt = NMT * m + j
        p = bt % 2
        hi3 = slot_hi[m % 2].rearrange("p (a b) -> p a b", a=KH)
        lo3 = slot_lo[m % 2].rearrange("p (a b) -> p a b", a=KH)
        sf3 = slotf.rearrange("p (a b) -> p a b", a=KH)
        qw("Y", "V", g_user[0])
        dma("Y", "WB", gt_t, g_sp[bt])
        b1_prefetch_ab(bt + 1)
        qw("V", "T", tfree)
        qw("V", f"WA{p}")
        for oc in range(KH):
            qi("V", "V", 1, lambda e, _oc=oc: e.tensor_tensor_scan(
                h_t[:, _oc * BT:(_oc + 1) * BT], at_t[p][:, _oc * BT:(_oc + 1) * BT],
                bt_t[p][:, _oc * BT:(_oc + 1) * BT], runB[:, _oc:_oc + 1],
                op0=ALU.mult, op1=ALU.add))
        ab_scan_user[p] = cnt["V"]
        qi("V", "V", 1, lambda e: e.tensor_copy(
            runB.rearrange("p (a b) -> p a b", a=KH),
            h_t.rearrange("p (a b) -> p a b", a=KH)[:, :, BT - 1:BT]))
        qw("V", "WB")
        qw("V", "S", slotf_hiS[0])
        g_user[0] = qi(
            "V", "V", 1, lambda e: e.tensor_tensor(slotf, gt_t, h_t, op=ALU.mult))
        qw("S", "V")
        qw("S", "T", tfree)
        slotf_hiS[0] = hs = qi(
            "S", "S", 1, lambda e, _d=hi3[:, :, j * BT:(j + 1) * BT], _s=sf3:
            e.activation(_d, _s, AF.Copy))
        qw("V", "S", hs)
        lv = qi("V", "V", 1, lambda e, _d=lo3[:, :, j * BT:(j + 1) * BT],
                _h=hi3[:, :, j * BT:(j + 1) * BT], _s=sf3:
                e.tensor_sub(_d, _s, _h))
        return hs, lv

    xop_user = [0, 0]
    pm_freeV = [cnt["V"]] * 4
    pH_freeV = cnt["V"]
    slot_readyS = [0, 0]
    slot_readyV = [0, 0]
    slot_freeT = [0, 0]
    spO_s = [0, 0]
    spO_v = [0, 0]
    sO_userT = [0, 0]

    def stage_split(t):
        q_ = t % 2
        qw("S", "W0")
        qw("S", "T", sO_userT[q_])
        spO_s[q_] = qi("S", "S", 1, lambda e, _q=q_: e.activation(
            sOhi[_q], stagO, AF.Copy))
        qw("V", "S", spO_s[q_])
        qw("V", "T", sO_userT[q_])
        spO_v[q_] = qi("V", "V", 1, lambda e, _q=q_: e.tensor_sub(
            sOlo[_q], stagO, sOhi[_q]))

    def bd_matvec(s):
        qw("T", "W0")
        qw("T", "V", pH_freeV)
        return qtg([(lambda e, _kt=kt, _s=s:
                     e.matmul(pH[:, _s:_s + 1], stagO[:, _kt * 128:(_kt + 1) * 128],
                              mb[:, _kt:_kt + 1], start=(_kt == 0),
                              stop=(_kt == KH - 1)))
                    for kt in range(KH)])

    b1_prefetch_ab(0)
    dma("Y", "W0", stagO, dwo[0])
    stage_split(0)
    mv = [0] * KD
    mv[0] = bd_matvec(0)
    qw("Y", "S", spO_s[0])
    qw("Y", "V", spO_v[0])
    qw("Y", "T", mv[0])
    dma("Y", "W0", stagO, dwo[1])
    for j in range(NMT):
        slot_readyS[0], slot_readyV[0] = b1_tile(0, j, 0)
    for m in range(NMT):
        hi_m = slot_hi[m % 2]
        lo_m = slot_lo[m % 2]
        for oc in range(KD):
            t = m * KD + oc
            q_ = t % 2
            nq = q_ ^ 1
            if t + 1 < NMT * KD:
                stage_split(t + 1)
                if t + 1 < KD:
                    mv[t + 1] = bd_matvec(t + 1)
            if t + 2 < NMT * KD:
                qw("Y", "S", spO_s[nq])
                qw("Y", "V", spO_v[nq])
                if t + 1 < KD:
                    qw("Y", "T", mv[t + 1])
                dma("Y", "W0", stagO, dwo[(t + 2) % KD])
            qw("T", "V", max(pm_freeV[oc % 4], slot_readyV[m % 2], spO_v[q_]))
            qw("T", "S", max(slot_readyS[m % 2], spO_s[q_]))
            sO_userT[q_] = mm3(
                pm[oc % 4], range(KH), sOhi[q_], sOlo[q_],
                lambda kt: hi_m[:, kt * MT:(kt + 1) * MT],
                lambda kt: lo_m[:, kt * MT:(kt + 1) * MT])
            tdone = cnt["T"]
            qw("G", "V", xop_user[oc % 2])
            dma("G", f"G{oc % 2}", xop[oc % 2],
                dx0[oc, :, HALO + m * MT: HALO + (m + 1) * MT])
            qw("V", "T", tdone)
            qw("V", f"G{oc % 2}")
            qw("V", f"PX{oc % 2}")
            vdone = qi("V", "V", 1, lambda e, _q=oc % 4, _oc=oc, _x=xop[oc % 2]:
                       e.scalar_tensor_tensor(
                           x1loc[:, _oc * MT:(_oc + 1) * MT], pm[_q],
                           ga_m[:, _oc:_oc + 1], _x, op0=ALU.mult, op1=ALU.add))
            pm_freeV[oc % 4] = vdone
            xop_user[oc % 2] = vdone
            qw("S", "V", vdone)
            dma("S", f"PX{oc % 2}", x1_sp[oc, :, m * MT:(m + 1) * MT],
                x1loc[:, oc * MT:(oc + 1) * MT])
            if oc % 2 == 1 and m + 1 < NMT:
                slot_readyS[(m + 1) % 2], slot_readyV[(m + 1) % 2] = b1_tile(
                    m + 1, (oc - 1) // 2, slot_freeT[(m + 1) % 2])
        slot_freeT[m % 2] = cnt["T"]
        if m == 0:
            qw("V", "T", mv[KD - 1])
            for oc in range(KD):
                qi("V", "V", 1, lambda e, _oc=oc: e.scalar_tensor_tensor(
                    x1b[:, _oc:_oc + 1], pH[:, _oc:_oc + 1], ga_m[:, _oc:_oc + 1],
                    x0b[:, _oc:_oc + 1], op0=ALU.mult, op1=ALU.add))
            x1bV = cnt["V"]
            qw("S", "V", x1bV)
            sqS = qi("S", "S", 1, lambda e: e.activation(tmp16a[:, 0:KD], x1b, AF.Square))
            qw("T", "S", sqS)
            qw("T", "V", x1bV)
            qtg([lambda e: e.matmul(pS[:, 0:KD], onescol, x1b, start=True, stop=True)])
            qtg([lambda e: e.matmul(pQ[:, 0:KD], onescol, tmp16a[:, 0:KD],
                                    start=True, stop=True)])
            qw("V", "T")
            qi("V", "V", 1, lambda e: e.reduce_sum(rowA[:, 0:1], pS[:, 0:KD], axis=AXX))
            qi("V", "V", 1, lambda e: e.reduce_sum(rowB[:, 0:1], pQ[:, 0:KD], axis=AXX))
            lnst["psq_user"] = cnt["V"]
            qi("V", "V", 1, lambda e: e.tensor_single_scalar(
                rowA[:, 1:2], rowA[:, 0:1], 1.0 / D, op=ALU.mult))
            qi("V", "V", 1, lambda e: e.tensor_single_scalar(
                rowB[:, 1:2], rowB[:, 0:1], 1.0 / D, op=ALU.mult))
            qw("S", "V")
            qi("S", "S", 1, lambda e: e.activation(rowC[:, 0:1], rowA[:, 1:2], AF.Square))
            qw("V", "S")
            qi("V", "V", 1, lambda e: e.tensor_sub(rowD[:, 0:1], rowB[:, 1:2], rowC[:, 0:1]))
            qw("S", "V")
            qi("S", "S", 1, lambda e: e.activation(rowC[:, 0:1], rowD[:, 0:1], AF.Sqrt, bias=eps1))
            qw("V", "S")
            qi("V", "V", 1, lambda e: e.reciprocal(rowB[:, 0:1], rowC[:, 0:1]))
            bdrows = qi("V", "V", 1, lambda e: e.tensor_single_scalar(
                rowC[:, 1:2], rowA[:, 1:2], -1.0, op=ALU.mult))
            qw("T", "V", bdrows)
            qtg([lambda e: e.matmul(pR[:, 0:1], ones1, rowB[:, 0:1], start=True, stop=True)])
            qtg([lambda e: e.matmul(pR[:, 1:2], ones1, rowC[:, 1:2], start=True, stop=True)])
            qw("V", "T")
            qi("V", "V", 1, lambda e: e.tensor_copy(tmp16a[:, 0:1], pR[:, 0:1]))
            qi("V", "V", 1, lambda e: e.tensor_copy(tmp16a[:, 1:2], pR[:, 1:2]))
            for oc in range(KD):
                qi("V", "V", 1, lambda e, _oc=oc: e.tensor_scalar(
                    tmp16b[:, 0:1], x1b[:, _oc:_oc + 1], tmp16a[:, 1:2], tmp16a[:, 0:1],
                    op0=ALU.add, op1=ALU.mult))
                qi("V", "V", 1, lambda e, _oc=oc: e.tensor_scalar(
                    tmp16b[:, 1:2], tmp16b[:, 0:1], cb1_m[:, _oc:_oc + 1],
                    sb_m[:, _oc:_oc + 1], op0=ALU.mult, op1=ALU.add))
                qi("V", "V", 1, lambda e, _oc=oc: e.tensor_scalar(
                    h2bcol[:, _oc:_oc + 1], tmp16b[:, 1:2], bmask_t, None, op0=ALU.mult))
        qw("S", "V")
        ln_stats(x1loc, MT, KD, repR3, repQ3, xsq3)
        for kt in range(KD):
            qi("V", "V", 1, lambda e, _x=x1loc[:, kt * MT:(kt + 1) * MT]:
               e.tensor_mul(uscr3, _x, repR3))
            qi("V", "V", 1, lambda e: e.tensor_add(uscr3b, uscr3, repQ3))
            qw("V", f"PH{kt % 2}")
            vdone = qi("V", "V", 1, lambda e, _k=kt, _h=h2piece[kt % 2]: e.tensor_scalar(
                _h, uscr3b, cb1_m[:, _k:_k + 1], sb_m[:, _k:_k + 1],
                op0=ALU.mult, op1=ALU.add))
            qw("S", "V", vdone)
            dma("S", f"PH{kt % 2}", h2_sp[kt, :, m * MT:(m + 1) * MT], h2piece[kt % 2])

    qbar()
    al.reset(PERSIST)
    h2_res = al.take(KD * (C + 1))
    b4mark = al.mark()
    xr_hi = takeH(KD * C)
    xr_lo = takeH(KD * C)
    xkspare = al.take(C)
    _xk_slots = [xkspare] + [h2_res[:, oc * (C + 1): oc * (C + 1) + C]
                             for oc in range(KD - 1)]
    xk_hi_aps = [s[:, 0:C // 2].bitcast(BF16) for s in _xk_slots]
    xk_lo_aps = [s[:, C // 2:C].bitcast(BF16) for s in _xk_slots]
    dbuf = al.take(C)
    bandtmp = [al.take(C), al.take(C)]
    rbuf = [al.take(MT) for _ in range(2)]
    krbuf = al.take(MT)
    ksq = [al.take(MT), al.take(MT)]
    fstag = al.take(KD * 128)
    fshi = [takeH(KD * 128) for _ in range(2)]
    fslo = [takeH(KD * 128) for _ in range(2)]
    fst = {"user": [0, 0], "sp_s": [0, 0], "sp_v": [0, 0],
           "st_s": 0, "st_v": 0}

    def fsplit(b):
        qw("S", f"W{b}")
        qw("S", "T", fst["user"][b])
        fst["sp_s"][b] = qi("S", "S", 1, lambda e, _b=b: e.activation(
            fshi[_b], fstag, AF.Copy))
        qw("V", "S", fst["sp_s"][b])
        qw("V", "T", fst["user"][b])
        fst["sp_v"][b] = qi("V", "V", 1, lambda e, _b=b: e.tensor_sub(
            fslo[_b], fstag, fshi[_b]))
        fst["st_s"] = fst["sp_s"][b]
        fst["st_v"] = fst["sp_v"][b]

    dma("Y", "W0", fstag, dwr[0])
    fsplit(0)

    mix_hiS = [0, 0]
    mix_userV = [0, 0]
    mix_hiS_arr = [0] * KD
    mix_loV_arr = [0] * KD
    qw("G", "P")
    for oc in range(KD):
        qw("G", "V", mix_userV[oc % 2])
        dma("G", f"G{oc % 2}", h2_res[:, oc * (C + 1) + 1: (oc + 1) * (C + 1)],
            h2_sp[oc])
        qw("V", f"G{oc % 2}")
        qi("V", "V", 1, lambda e, _oc=oc: e.tensor_copy(
            h2_res[:, _oc * (C + 1):_oc * (C + 1) + 1], h2bcol[:, _oc:_oc + 1]))
        base = oc * (C + 1)
        qi("V", "V", 1, lambda e, _b=base: e.tensor_sub(
            dbuf, h2_res[:, _b + 1:_b + 1 + C], h2_res[:, _b:_b + C]))
        qw("V", "S", mix_hiS[oc % 2])
        st = qi("V", "V", 1, lambda e, _b=base, _oc=oc, _t=bandtmp[oc % 2]:
                e.scalar_tensor_tensor(
                    _t, dbuf, mur_t[:, _oc:_oc + 1],
                    h2_res[:, _b:_b + C], op0=ALU.mult, op1=ALU.add))
        qw("S", "V", st)
        mix_hiS[oc % 2] = hs = qi(
            "S", "S", 1, lambda e, _oc=oc, _t=bandtmp[oc % 2]: e.activation(
                xr_hi[:, _oc * C:(_oc + 1) * C], _t, AF.Copy))
        mix_hiS_arr[oc] = hs
        qw("V", "S", hs)
        mix_userV[oc % 2] = mix_loV_arr[oc] = qi(
            "V", "V", 1, lambda e, _oc=oc, _t=bandtmp[oc % 2]: e.tensor_sub(
                xr_lo[:, _oc * C:(_oc + 1) * C], _t,
                xr_hi[:, _oc * C:(_oc + 1) * C]))

    def mix_k_band(oc):
        base = oc * (C + 1)
        qi("V", "V", 1, lambda e, _b=base: e.tensor_sub(
            dbuf, h2_res[:, _b + 1:_b + 1 + C], h2_res[:, _b:_b + C]))
        qw("V", "S", mix_hiS[oc % 2])
        st = qi("V", "V", 1, lambda e, _b=base, _oc=oc, _t=bandtmp[oc % 2]:
                e.scalar_tensor_tensor(
                    _t, dbuf, muk_t[:, _oc:_oc + 1],
                    h2_res[:, _b:_b + C], op0=ALU.mult, op1=ALU.add))
        qw("S", "V", st)
        mix_hiS[oc % 2] = hs = qi(
            "S", "S", 1, lambda e, _oc=oc, _t=bandtmp[oc % 2]: e.activation(
                xk_hi_aps[_oc], _t, AF.Copy))
        qw("V", "S", hs)
        qi("V", "V", 1, lambda e, _oc=oc, _t=bandtmp[oc % 2]: e.tensor_sub(
            xk_lo_aps[_oc], _t, xk_hi_aps[_oc]))

    state = {"pm_freeV": [0] * 4, "pm_freeS": [0] * 4}

    def ffn_sweep(wsl, rhs_hi_fn, rhs_lo_fn, evac, filler=None,
                  prestaged=False, band_waits=None, pmV=None, pmS=None):
        if pmV is None:
            qbar()
        state["pm_freeV"] = list(pmV) if pmV is not None else [cnt["V"]] * 4
        state["pm_freeS"] = list(pmS) if pmS is not None else [cnt["S"]] * 4
        if not prestaged:
            qw("Y", "S", fst["st_s"])
            qw("Y", "V", fst["st_v"])
            dma("Y", "W0", fstag, wsl[0])
            fsplit(0)
        qw("Y", "S", fst["st_s"])
        qw("Y", "V", fst["st_v"])
        dma("Y", "W1", fstag, wsl[1])
        for oc in range(KD):
            buf = oc % 2
            nbuf = buf ^ 1
            if oc + 1 < KD:
                fsplit(nbuf)
            if oc + 2 < KD:
                qw("Y", "S", fst["st_s"])
                qw("Y", "V", fst["st_v"])
                dma("Y", f"W{buf}", fstag, wsl[oc + 2])
            for m in range(NMT):
                qw("T", "V", max(state["pm_freeV"][m], fst["sp_v"][buf]))
                qw("T", "S", max(state["pm_freeS"][m], fst["sp_s"][buf]))
                if band_waits is not None and oc == 0:
                    hiA, loA = band_waits
                    for kt in range(KD):
                        qw("T", "S", hiA[kt])
                        qw("T", "V", loA[kt])
                        fst["user"][buf] = mm3(
                            pm[m], [kt], fshi[buf], fslo[buf],
                            lambda k2, _m=m: rhs_hi_fn(k2, _m),
                            lambda k2, _m=m: rhs_lo_fn(k2, _m),
                            first=(kt == 0), stop=(kt == KD - 1))
                else:
                    fst["user"][buf] = mm3(
                        pm[m], range(KD), fshi[buf], fslo[buf],
                        lambda kt, _m=m: rhs_hi_fn(kt, _m),
                        lambda kt, _m=m: rhs_lo_fn(kt, _m))
                evac(oc, m, cnt["T"])
            if filler is not None:
                filler(oc)

    def evac_r(oc, m, tdone):
        qw("S", "T", tdone)
        qw("S", "P")
        qi("S", "S", 1, lambda e, _m=m, _o=rbuf[m % 2]: e.activation(_o, pm[_m], AF.Sigmoid))
        state["pm_freeS"][m] = cnt["S"]
        dma("S", "P", r_sp[oc, :, m * MT:(m + 1) * MT], rbuf[m % 2])

    ffn_sweep([dwr[i] for i in range(KD)],
              lambda kt, m: xr_hi[:, kt * C + m * MT: kt * C + (m + 1) * MT],
              lambda kt, m: xr_lo[:, kt * C + m * MT: kt * C + (m + 1) * MT],
              evac_r, filler=mix_k_band, prestaged=True,
              band_waits=(mix_hiS_arr, mix_loV_arr),
              pmV=pm_freeV, pmS=[fst["sp_s"][0]] * 4)

    kact_hi = xr_hi
    kact_lo = xr_lo
    ksq_loV = [0, 0]

    def evac_k(oc, m, tdone):
        qw("V", "T", tdone)
        qw("V", "S")
        state["pm_freeV"][m] = qi("V", "V", 1, lambda e, _m=m: e.tensor_relu(krbuf, pm[_m]))
        qw("S", "V")
        qi("S", "S", 1, lambda e, _m=m: e.activation(ksq[_m % 2], krbuf, AF.Square))
        hs = qi("S", "S", 1, lambda e, _oc=oc, _m=m: e.activation(
            kact_hi[:, _oc * C + _m * MT: _oc * C + (_m + 1) * MT],
            ksq[_m % 2], AF.Copy))
        qw("V", "S", hs)
        ksq_loV[m % 2] = qi("V", "V", 1, lambda e, _oc=oc, _m=m: e.tensor_sub(
            kact_lo[:, _oc * C + _m * MT: _oc * C + (_m + 1) * MT], ksq[_m % 2],
            kact_hi[:, _oc * C + _m * MT: _oc * C + (_m + 1) * MT]))

    ffn_sweep([dwk[i] for i in range(KD)],
              lambda kt, m: xk_hi_aps[kt][:, m * MT:(m + 1) * MT],
              lambda kt, m: xk_lo_aps[kt][:, m * MT:(m + 1) * MT],
              evac_k)

    ebuf = dbuf[:, 0:MT]
    x2buf = [dbuf[:, MT:2 * MT], dbuf[:, 2 * MT:3 * MT]]
    rpc = [dbuf[:, 3 * MT:4 * MT], bandtmp[0][:, 0:MT]]
    x1pc = [bandtmp[0][:, MT:2 * MT], bandtmp[0][:, 2 * MT:3 * MT]]
    pc_user = [0, 0]

    def evac_v(oc, m, tdone):
        qw("G", "V", pc_user[m % 2])
        dma("G", f"G{m % 2}", rpc[m % 2], r_sp[oc, :, m * MT:(m + 1) * MT])
        dma("G", f"G{m % 2}", x1pc[m % 2], x1_sp[oc, :, m * MT:(m + 1) * MT])
        qw("V", "T", tdone)
        qw("V", f"G{m % 2}")
        qw("V", "GO")
        state["pm_freeV"][m] = qi("V", "V", 1, lambda e, _m=m: e.tensor_mul(
            ebuf, rpc[_m % 2], pm[_m]))
        vd = qi("V", "V", 1, lambda e, _m=m, _oc=oc: e.scalar_tensor_tensor(
            x2buf[_m % 2], ebuf, gb_m[:, _oc:_oc + 1], x1pc[_m % 2],
            op0=ALU.mult, op1=ALU.add))
        pc_user[m % 2] = vd
        qw("G", "V", vd)
        dma("G", "GO", dy[oc, :, m * MT:(m + 1) * MT], x2buf[m % 2])

    ffn_sweep([dwv[i] for i in range(KD)],
              lambda kt, m: kact_hi[:, kt * C + m * MT: kt * C + (m + 1) * MT],
              lambda kt, m: kact_lo[:, kt * C + m * MT: kt * C + (m + 1) * MT],
              evac_v)

    qw("G", "GO")
    qw("G", "G")

    for k in maxwait:
        assert maxwait[k] <= cnt[k], (k, maxwait[k], cnt[k])

    with nc.Block() as block:
        block.tensor(lambda e: [f(e) for f in Q["T"]])
        block.vector(lambda e: [f(e) for f in Q["V"]])
        block.scalar(lambda e: [f(e) for f in Q["S"]])
        block.sync(lambda e: [f(e) for f in Q["Y"]])
        block.gpsimd(lambda e: [f(e) for f in Q["G"]])
    es.close()
    return nc


def _slabs(w, koct, kkt):
    o, i = w.shape
    assert o == koct * 128 and i == kkt * 128
    return np.ascontiguousarray(
        w.reshape(koct, 128, kkt, 128).transpose(0, 3, 2, 1).reshape(koct, 128, kkt * 128))


def _pcol(v, n):
    return np.ascontiguousarray(v.reshape(n, 128).T)


def kernel(L0_attn_x, L0_attn_W_in, L0_attn_conv_w, L0_attn_conv_b, L0_attn_W_g,
           L0_attn_b_g, L0_attn_forget_base, L0_attn_W_out, L0_ffn_W_r, L0_ffn_W_k,
           L0_ffn_W_v, L0_ffn_mu_r, L0_ffn_mu_k, L0_c, L0_wmod):
    f32 = np.float32
    x = np.asarray(L0_attn_x, f32)[0]
    XT = np.zeros((D, T_FULL + HALO), f32)
    XT[:, HALO:] = np.ascontiguousarray(x.T)

    win_sl = _slabs(np.asarray(L0_attn_W_in, f32), 32, KD)
    wg_sl = _slabs(np.asarray(L0_attn_W_g, f32), 32, KH)
    wo_sl = _slabs(np.asarray(L0_attn_W_out, f32), KD, KH)
    wr_sl = _slabs(np.asarray(L0_ffn_W_r, f32), KD, KD)
    wk_sl = _slabs(np.asarray(L0_ffn_W_k, f32), KD, KD)
    wv_sl = _slabs(np.asarray(L0_ffn_W_v, f32), KD, KD)
    wmod = np.asarray(L0_wmod, f32)
    cvec = _pcol(np.asarray(L0_c, f32)[0], KD)
    bg = np.asarray(L0_attn_b_g, f32)
    convw = np.ascontiguousarray(np.stack(
        [_pcol(np.ascontiguousarray(np.asarray(L0_attn_conv_w, f32)[:, 0, k]), KH)
         for k in range(4)]))
    convb = _pcol(np.asarray(L0_attn_conv_b, f32), KH)
    bgf = _pcol(np.ascontiguousarray(bg[:HID]), KH)
    bgi = _pcol(np.ascontiguousarray(bg[HID:]), KH)
    fb = _pcol(np.asarray(L0_attn_forget_base, f32), KH)
    mur = _pcol(np.asarray(L0_ffn_mu_r, f32), KD)
    muk = _pcol(np.asarray(L0_ffn_mu_k, f32), KD)

    in_maps = []
    for i in range(NCORE):
        x0 = np.ascontiguousarray(XT[:, i * C:i * C + CH].reshape(KD, 128, CH))
        wmod_i = np.ascontiguousarray(
            wmod[:, i * MODS:(i + 1) * MODS].reshape(KD, 128, MODS))
        bmask = np.full((128, 1), 0.0 if i == 0 else 1.0, f32)
        cm = np.zeros((7, 128, KH), f32)
        cm[:i] = 1.0
        cm1 = np.ascontiguousarray(1.0 - cm)
        in_maps.append({
            "x0": x0, "w_in_sl": win_sl, "w_g_sl": wg_sl, "w_out_sl": wo_sl,
            "w_r_sl": wr_sl, "w_k_sl": wk_sl, "w_v_sl": wv_sl, "wmod_sl": wmod_i,
            "cvec": cvec, "conv_w4": convw, "conv_b": convb, "bg_f": bgf,
            "bg_i": bgi, "fb": fb, "mu_r": mur, "mu_k": muk, "bmask": bmask,
            "cmask": cm, "cmask1": cm1,
        })

    nc = build_nc()
    res = run_bass_kernel_spmd(nc, in_maps, core_ids=list(range(NCORE)))
    out = np.empty((1, T_FULL, D), f32)
    for i in range(NCORE):
        y = res.results[i]["y"]
        out[0, i * C:(i + 1) * C, :] = np.asarray(y).transpose(2, 0, 1).reshape(C, D)
    return out


# version 29 (final)
# speedup vs baseline: 1.0044x; 1.0044x over previous
import numpy as np
from contextlib import ExitStack
import concourse.bass as bass
import concourse.mybir as mybir
from concourse.bass_utils import run_bass_kernel_spmd

F32 = mybir.dt.float32
BF16 = mybir.dt.bfloat16
U8 = mybir.dt.uint8
AF = mybir.ActivationFunctionType
ALU = mybir.AluOpType
AXX = mybir.AxisListType.X

NCORE = 8
D = 1024
HID = 2048
T_FULL = 16384
C = 2048
HALO = 4
CH = C + HALO
KD = D // 128
KH = HID // 128
MT = 512
NMT = C // MT
BT = 128
NBT = C // BT
MODS = 6 * D // NCORE
EPS = 1e-5

ARENA_COLS = 49152
DBG_NO_CC = False


def build_nc():
    nc = bass.Bass("TRN2", target_bir_lowering=False)
    es = ExitStack()

    dx0 = nc.dram_tensor("x0", [KD, 128, CH], F32, kind="ExternalInput").ap()
    dwin = nc.dram_tensor("w_in_sl", [32, 128, KD * 128], F32, kind="ExternalInput").ap()
    dwg = nc.dram_tensor("w_g_sl", [32, 128, KH * 128], F32, kind="ExternalInput").ap()
    dwo = nc.dram_tensor("w_out_sl", [KD, 128, KH * 128], F32, kind="ExternalInput").ap()
    dwr = nc.dram_tensor("w_r_sl", [KD, 128, KD * 128], F32, kind="ExternalInput").ap()
    dwk = nc.dram_tensor("w_k_sl", [KD, 128, KD * 128], F32, kind="ExternalInput").ap()
    dwv = nc.dram_tensor("w_v_sl", [KD, 128, KD * 128], F32, kind="ExternalInput").ap()
    dwm = nc.dram_tensor("wmod_sl", [KD, 128, MODS], F32, kind="ExternalInput").ap()
    dcvec = nc.dram_tensor("cvec", [128, KD], F32, kind="ExternalInput").ap()
    dconvw = nc.dram_tensor("conv_w4", [4, 128, KH], F32, kind="ExternalInput").ap()
    dconvb = nc.dram_tensor("conv_b", [128, KH], F32, kind="ExternalInput").ap()
    dbgf = nc.dram_tensor("bg_f", [128, KH], F32, kind="ExternalInput").ap()
    dbgi = nc.dram_tensor("bg_i", [128, KH], F32, kind="ExternalInput").ap()
    dfb = nc.dram_tensor("fb", [128, KH], F32, kind="ExternalInput").ap()
    dmur = nc.dram_tensor("mu_r", [128, KD], F32, kind="ExternalInput").ap()
    dmuk = nc.dram_tensor("mu_k", [128, KD], F32, kind="ExternalInput").ap()
    dbmask = nc.dram_tensor("bmask", [128, 1], F32, kind="ExternalInput").ap()
    dcm = nc.dram_tensor("cmask", [7, 128, KH], F32, kind="ExternalInput").ap()
    dcm1 = nc.dram_tensor("cmask1", [7, 128, KH], F32, kind="ExternalInput").ap()
    dy = nc.dram_tensor("y", [KD, 128, C], F32, kind="ExternalOutput").ap()

    g_sp = nc.dram_tensor("g_sp", [NBT, 128, KH * BT], F32).ap()
    c_hi_sp = nc.dram_tensor("c_hi_sp", [KH, 128, C], BF16).ap()
    c_lo_sp = nc.dram_tensor("c_lo_sp", [KH, 128, C], BF16).ap()
    a_sp = nc.dram_tensor("a_sp", [NBT, 128, KH * BT], F32).ap()
    b_sp = nc.dram_tensor("b_sp", [NBT, 128, KH * BT], F32).ap()
    x1_sp = nc.dram_tensor("x1_sp", [KD, 128, C], F32).ap()
    r_sp = nc.dram_tensor("r_sp", [KD, 128, C], F32).ap()
    h2_sp = nc.dram_tensor("h2_sp", [KD, 128, C], F32).ap()
    ccm_in = nc.dram_tensor("ccm_in", [1, MODS], F32)
    ccm_out = nc.dram_tensor("ccm_out", [NCORE, MODS], F32)
    ccc_in = nc.dram_tensor("ccc_in", [128, 2 * KH], F32)
    ccc_out = nc.dram_tensor("ccc_out", [NCORE, 128, 2 * KH], F32)

    AR = es.enter_context(nc.sbuf_tensor("arena", [128, ARENA_COLS], F32)).ap()
    ones1 = es.enter_context(nc.sbuf_tensor("ones1", [1, 128], F32)).ap()
    onescol = es.enter_context(nc.sbuf_tensor("onescol", [128, 1], F32)).ap()
    rowA = es.enter_context(nc.sbuf_tensor("rowA", [1, MT], F32)).ap()
    rowB = es.enter_context(nc.sbuf_tensor("rowB", [1, MT], F32)).ap()
    rowC = es.enter_context(nc.sbuf_tensor("rowC", [1, MT], F32)).ap()
    rowD = es.enter_context(nc.sbuf_tensor("rowD", [1, MT], F32)).ap()
    rowM = es.enter_context(nc.sbuf_tensor("rowM", [1, MODS], F32)).ap()
    eps1 = es.enter_context(nc.sbuf_tensor("eps1", [1, 1], F32)).ap()
    selm = es.enter_context(nc.sbuf_tensor("selm", [128, KD], U8)).ap()

    pm = [es.enter_context(nc.psum_tensor(f"pm{i}", [128, MT], F32)).ap()
          for i in range(4)]
    pH = es.enter_context(nc.psum_tensor("pH", [128, 16], F32)).ap()
    pS = es.enter_context(nc.psum_tensor("pS", [1, MT], F32)).ap()
    pQ = es.enter_context(nc.psum_tensor("pQ", [1, MT], F32)).ap()
    pR = es.enter_context(nc.psum_tensor("pR", [128, MT], F32)).ap()

    sems = {}
    for k in ("T", "V", "S", "GC", "W0", "W1", "WB", "WA0", "WA1", "P", "PA0",
              "PA1", "PB0", "PB1", "PH0", "PH1", "PX0", "PX1", "G", "G0", "G1",
              "GO", "CC"):
        sems[k] = es.enter_context(nc.semaphore(f"sem{k}"))

    Q = {k: [] for k in ("T", "V", "S", "Y", "G")}
    DMAS = ("W0", "W1", "WB", "WA0", "WA1", "P", "PA0", "PA1", "PB0", "PB1",
            "PH0", "PH1", "PX0", "PX1", "G", "G0", "G1", "GO", "CC")
    cnt = {k: 0 for k in ("T", "V", "S", "GC") + DMAS}
    maxwait = {k: 0 for k in cnt}

    def q(stream, fn):
        Q[stream].append(fn)

    def qw(stream, semk, val=None):
        if semk in DMAS:
            val = None
        v = cnt[semk] if val is None else val
        assert v <= cnt[semk]
        maxwait[semk] = max(maxwait[semk], v)
        tgt = sems[semk]
        q(stream, lambda e, _t=tgt, _v=v: e.wait_ge(_t, _v))

    def qi(stream, semk, n, fn):
        if stream == "V":
            qw("V", "V")
        elif stream == "S":
            qw("S", "S")
        elif stream == "G" and semk == "GC":
            qw("G", "GC")
        cnt[semk] += n
        tgt = sems[semk]
        q(stream, lambda e, _f=fn, _t=tgt, _n=n: _f(e).then_inc(_t, _n))
        return cnt[semk]

    def qn(stream, fn):
        q(stream, fn)

    def qtg(fns):
        for f in fns[:-1]:
            qn("T", f)
        return qi("T", "T", 1, fns[-1])

    def qbar():
        own = {"T": ("T",), "V": ("V",),
               "S": ("S", "P", "PA0", "PA1", "PB0", "PB1", "PH0", "PH1",
                     "PX0", "PX1"),
               "Y": ("W0", "W1", "WB", "WA0", "WA1"),
               "G": ("GC", "G", "G0", "G1", "GO", "CC")}
        for stream in ("T", "V", "S", "Y", "G"):
            mine = own[stream]
            for semk in ("T", "V", "S", "GC", "W0", "W1", "WB", "WA0", "WA1", "P",
                         "PA0", "PA1", "PB0", "PB1", "PH0", "PH1", "PX0", "PX1",
                         "G", "G0", "G1", "GO"):
                if semk in mine or cnt[semk] == 0:
                    continue
                qw(stream, semk)

    class Alloc:
        def __init__(self):
            self.off = 0
            self.peak = 0
        def take(self, n):
            a = self.off
            self.off += n
            self.peak = max(self.peak, self.off)
            assert self.off <= ARENA_COLS, f"arena overflow {self.off}"
            return AR[:, a:a + n]
        def mark(self):
            return self.off
        def reset(self, m):
            self.off = m

    al = Alloc()

    def takeH(n):
        return al.take((n + 1) // 2).bitcast(BF16)

    runA = al.take(KH)
    runB = al.take(KH)
    Lsum = al.take(KH)
    accb = al.take(4)
    gate_b = al.take(KH)
    carA = al.take(KH)
    carB = al.take(KH)
    ab_sb = al.take(2 * KH)
    aball = al.take(NCORE * 2 * KH)
    svec = al.take(KD)
    sa_m = al.take(KD); ca1_m = al.take(KD); ga_m = al.take(KD)
    sb_m = al.take(KD); cb1_m = al.take(KD); gb_m = al.take(KD)
    rawca = al.take(KD); rawcb = al.take(KD)
    cvec_t = al.take(KD)
    convw_t = [al.take(KH) for _ in range(4)]
    convb_t = al.take(KH)
    bgf_t = al.take(KH); bgi_t = al.take(KH); fb_t = al.take(KH)
    kvec = al.take(KH)
    mur_t = al.take(KD); muk_t = al.take(KD)
    bmask_t = al.take(1)
    cm_t = al.take(7 * KH); cm1_t = al.take(7 * KH)
    x0b = al.take(KD)
    x1b = al.take(KD)
    mb = al.take(KH)
    h2bcol = al.take(KD)
    tmp16a = al.take(KH); tmp16b = al.take(KH); tmp16c = al.take(KH)
    selsrcA = al.take(KD); selsrcB = al.take(KD)
    PERSIST = al.mark()

    def dma(stream, semk, out_ap, in_ap, slow=False):
        if slow:
            return qi(stream, semk, 16,
                      lambda e, _o=out_ap, _i=in_ap: e.dma_start(
                          out=_o, in_=_i, allow_slow_non_contiguous=True))
        return qi(stream, semk, 16,
                  lambda e, _o=out_ap, _i=in_ap: e.dma_start(out=_o, in_=_i))

    lnst = {"psq_user": 0}

    def ln_stats(xbuf, L, kt_n, repR, repQ, xsqp):
        nch = kt_n * 128
        qw("T", "V", lnst["psq_user"])
        qtg([(lambda e, _r=xbuf[:, kt * L:(kt + 1) * L], _st=(kt == 0),
              _sp=(kt == kt_n - 1), _L=L:
              e.matmul(pS[:, :_L], onescol, _r, start=_st, stop=_sp))
             for kt in range(kt_n)])
        mmdone = []
        for kt in range(kt_n):
            if kt >= 2:
                qw("S", "T", mmdone[kt - 2])
            sqd = qi("S", "S", 1, lambda e, _x=xbuf[:, kt * L:(kt + 1) * L],
                     _q=xsqp[kt % 2], _L=L: e.activation(_q[:, :_L], _x, AF.Square))
            qw("T", "S", sqd)
            mmdone.append(qi("T", "T", 1, lambda e, _q=xsqp[kt % 2], _L=L,
                             _st=(kt == 0), _sp=(kt == kt_n - 1):
                             e.matmul(pQ[:, :_L], onescol, _q[:, :_L],
                                      start=_st, stop=_sp)))
        qw("V", "T")
        inv = 1.0 / nch
        qi("V", "V", 1, lambda e, _L=L: e.tensor_single_scalar(
            rowA[:, :_L], pS[:, :_L], inv, op=ALU.mult))
        lnst["psq_user"] = qi("V", "V", 1, lambda e, _L=L: e.tensor_single_scalar(
            rowB[:, :_L], pQ[:, :_L], inv, op=ALU.mult))
        qw("S", "V")
        qi("S", "S", 1, lambda e, _L=L: e.activation(rowC[:, :_L], rowA[:, :_L], AF.Square))
        qw("V", "S")
        qi("V", "V", 1, lambda e, _L=L: e.tensor_sub(rowD[:, :_L], rowB[:, :_L], rowC[:, :_L]))
        qw("S", "V")
        qi("S", "S", 1, lambda e, _L=L: e.activation(rowC[:, :_L], rowD[:, :_L], AF.Sqrt, bias=eps1))
        qw("V", "S")
        qi("V", "V", 1, lambda e, _L=L: e.reciprocal(rowB[:, :_L], rowC[:, :_L]))
        qi("V", "V", 1, lambda e, _L=L: e.tensor_mul(rowC[:, :_L], rowA[:, :_L], rowB[:, :_L]))
        qi("V", "V", 1, lambda e, _L=L: e.tensor_single_scalar(
            rowD[:, :_L], rowC[:, :_L], -1.0, op=ALU.mult))
        qw("T", "V")
        qtg([lambda e, _L=L: e.matmul(pR[:, :_L], ones1, rowB[:, :_L], start=True, stop=True)])
        qw("V", "T")
        qi("V", "V", 1, lambda e, _L=L, _r=repR: e.tensor_copy(_r[:, :_L], pR[:, :_L]))
        qw("T", "V")
        qtg([lambda e, _L=L: e.matmul(pR[:, :_L], ones1, rowD[:, :_L], start=True, stop=True)])
        qw("V", "T")
        qi("V", "V", 1, lambda e, _L=L, _r=repQ: e.tensor_copy(_r[:, :_L], pR[:, :_L]))

    dma("Y", "W0", cvec_t, dcvec)
    for k in range(4):
        dma("Y", "W0", convw_t[k], dconvw[k])
    dma("Y", "W0", convb_t, dconvb)
    dma("Y", "W0", bgf_t, dbgf)
    dma("Y", "W0", bgi_t, dbgi)
    dma("Y", "W0", fb_t, dfb)
    dma("Y", "W0", mur_t, dmur)
    dma("Y", "W0", muk_t, dmuk)
    dma("Y", "W0", bmask_t, dbmask)
    for r in range(7):
        dma("Y", "W0", cm_t[:, r * KH:(r + 1) * KH], dcm[r])
        dma("Y", "W0", cm1_t[:, r * KH:(r + 1) * KH], dcm1[r])

    m0mark = al.mark()
    wm_sl = [al.take(MODS) for _ in range(KD)]
    for kt in range(KD):
        dma("Y", "W1", wm_sl[kt], dwm[kt])

    qi("V", "V", 1, lambda e: e.memset(ones1, 1.0))
    qi("V", "V", 1, lambda e: e.memset(eps1, EPS))
    qi("V", "V", 1, lambda e: e.memset(onescol, 1.0))
    qi("V", "V", 1, lambda e: e.memset(runA, 0.0))

    qw("S", "W0")
    qw("S", "V")
    qi("S", "S", 1, lambda e: e.activation(svec, cvec_t, AF.Silu))
    qi("S", "S", 1, lambda e: e.activation(tmp16a, fb_t, AF.Exp))
    qi("S", "S", 1, lambda e: e.activation(tmp16b, tmp16a, AF.Ln, bias=onescol))
    qw("V", "S")
    qi("V", "V", 1, lambda e: e.tensor_single_scalar(kvec, tmp16b, -8.0, op=ALU.mult))

    qw("T", "W1")
    qw("T", "S")
    n512 = MODS // MT
    segs = [(i * MT, MT) for i in range(n512)]
    if MODS - n512 * MT:
        segs.append((n512 * MT, MODS - n512 * MT))
    for si, (o, L) in enumerate(segs):
        ps = pS if si == 0 else pQ
        qtg([(lambda e, _p=ps, _kt=kt, _o=o, _L=L:
              e.matmul(_p[:, :_L], svec[:, _kt:_kt + 1], wm_sl[_kt][:, _o:_o + _L],
                       start=(_kt == 0), stop=(_kt == KD - 1)))
             for kt in range(KD)])
    qw("V", "T")
    for si, (o, L) in enumerate(segs):
        ps = pS if si == 0 else pQ
        qi("V", "V", 1, lambda e, _p=ps, _o=o, _L=L:
           e.tensor_copy(rowM[:, _o:_o + _L], _p[:, :_L]))
    lnst["psq_user"] = cnt["V"]
    qw("G", "V")
    dma("G", "G", ccm_in.ap(), rowM)
    qw("G", "G")
    if DBG_NO_CC:
        for _r in range(NCORE):
            dma("G", "G", ccm_out.ap()[_r:_r + 1, :], ccm_in.ap())
        qw("G", "G")
    else:
        qi("G", "CC", 1, lambda e: e.collective_compute(
            "AllGather", ALU.bypass, replica_groups=[list(range(NCORE))],
            ins=[ccm_in.ap().opt()], outs=[ccm_out.ap().opt()]))
        qw("G", "CC")
    modflat = ccm_out.ap().rearrange("a b -> (a b)")
    for v, dst in [(0, sa_m), (1, rawca), (2, ga_m), (3, sb_m), (4, rawcb), (5, gb_m)]:
        src = modflat[v * D:(v + 1) * D].rearrange("(kt p) -> p kt", p=128)
        dma("G", "G", dst, src, slow=True)
    dma("G", "G", x0b, dx0.transpose([1, 0, 2])[:, :, 3:4].rearrange("p a b -> p (a b)"),
        slow=True)

    al.reset(m0mark)

    h1hi = takeH(KD * CH)
    h1lo = takeH(KD * CH)
    a0mark = al.mark()
    h1f = al.take(KD * MT)
    x0bf = [al.take(KD * MT) for _ in range(2)]
    repR_t = [al.take(MT) for _ in range(NMT + 1)]
    repQ_t = [al.take(MT) for _ in range(NMT + 1)]
    uscr0 = al.take(MT)
    uscr0b = al.take(MT)
    xsq0 = [al.take(MT) for _ in range(2)]
    tiles_a0 = [(0, HALO)] + [(HALO + m * MT, MT) for m in range(NMT)]
    h1doneV = {}
    h1doneS = {}
    x0_userV = [0, 0]
    x0_userS = [0, 0]
    x0_userT = [0, 0]
    h1hi3 = h1hi.rearrange("p (a b) -> p a b", a=KD)
    h1lo3 = h1lo.rearrange("p (a b) -> p a b", a=KD)
    for ti, (off, L) in enumerate(tiles_a0):
        xb = x0bf[ti % 2]
        qw("Y", "V", x0_userV[ti % 2])
        qw("Y", "S", x0_userS[ti % 2])
        qw("Y", "T", x0_userT[ti % 2])
        dma("Y", f"WA{ti % 2}", xb[:, :KD * L].rearrange("p (a b) -> p a b", a=KD),
            dx0.transpose([1, 0, 2])[:, :, off:off + L])
        qw("V", f"WA{ti % 2}")
        qw("S", f"WA{ti % 2}")
        qw("T", f"WA{ti % 2}")
        ln_stats(xb[:, :KD * L], L, KD, repR_t[ti], repQ_t[ti], xsq0)
        x0_userS[ti % 2] = cnt["S"]
        x0_userT[ti % 2] = cnt["T"]
        x0_userV[ti % 2] = cnt["V"]

    qw("V", "G")
    qi("V", "V", 1, lambda e: e.tensor_single_scalar(ca1_m, rawca, 1.0, op=ALU.add))
    qi("V", "V", 1, lambda e: e.tensor_single_scalar(cb1_m, rawcb, 1.0, op=ALU.add))
    for g in (ga_m, gb_m):
        qi("V", "V", 1, lambda e, _g=g: e.tensor_single_scalar(selm, _g, 0.0, op=ALU.is_ge))
        qi("V", "V", 1, lambda e, _g=g: e.tensor_single_scalar(selsrcA, _g, 0.25, op=ALU.max))
        qi("V", "V", 1, lambda e, _g=g: e.tensor_single_scalar(selsrcB, _g, -0.25, op=ALU.min))
        qi("V", "V", 1, lambda e, _g=g: e.select(_g, selm, selsrcA, selsrcB, add_drain=True))

    a1mark = al.mark()
    stagW = [al.take(KD * 128) for _ in range(2)]
    slabWhi = [takeH(KD * 128) for _ in range(2)]
    slabWlo = [takeH(KD * 128) for _ in range(2)]
    xcb = al.take(HALO + C)
    cA = h1f[:, 0:C]
    cB = al.take(C)
    chis = [h1f[:, C + i * (C // 2): C + (i + 1) * (C // 2)].bitcast(BF16)
            for i in range(2)]
    clos = [takeH(C), takeH(C)]
    gbufs = [al.take(C), cB]
    htmp = al.take(HALO)
    slabW_userT = [0, 0]
    stag_userS = [0, 0]
    stag_userV = [0, 0]
    split_S = [0, 0]
    split_V = [0, 0]
    pm_freeV = [0, 0, 0, 0]
    pm_freeS = [0, 0, 0, 0]
    pH_freeV = 0
    pH_freeS = 0

    def mm3(ps, kts, whi, wlo, rhi_fn, rlo_fn, first=True, stop=True):
        trip = []
        for kt in kts:
            trip.append((whi[:, kt * 128:(kt + 1) * 128], rhi_fn(kt)))
            trip.append((whi[:, kt * 128:(kt + 1) * 128], rlo_fn(kt)))
            trip.append((wlo[:, kt * 128:(kt + 1) * 128], rhi_fn(kt)))
        last = len(trip) - 1
        return qtg([(lambda e, _w=w, _r=r, _st=(first and i == 0),
                     _sp=(stop and i == last), _p=ps:
                     e.matmul(_p, _w, _r, start=_st, stop=_sp))
                    for i, (w, r) in enumerate(trip)])

    oc_order = list(range(16, 32)) + list(range(16))
    slabW_userT = [0, 0]
    dma("Y", "W0", stagW[0], dwin[oc_order[0]])
    dma("Y", "W1", stagW[1], dwin[oc_order[1]])
    qw("S", "W0")
    split_S[0] = stag_userS[0] = qi(
        "S", "S", 1, lambda e: e.activation(slabWhi[0], stagW[0], AF.Copy))
    qw("V", "S", split_S[0])
    split_V[0] = stag_userV[0] = qi(
        "V", "V", 1, lambda e: e.tensor_sub(slabWlo[0], stagW[0], slabWhi[0]))
    cB_freeV = 0

    for ti, (off, L) in enumerate(tiles_a0):
        xb = x0bf[ti % 2]
        qw("Y", "V", x0_userV[ti % 2])
        qw("Y", "S", x0_userS[ti % 2])
        qw("Y", "T", x0_userT[ti % 2])
        dma("Y", f"WA{ti % 2}", xb[:, :KD * L].rearrange("p (a b) -> p a b", a=KD),
            dx0.transpose([1, 0, 2])[:, :, off:off + L])
        qw("V", f"WA{ti % 2}")
        if ti > 0:
            qw("V", "S", h1doneS[ti - 1])
            qw("V", "GC", h1doneV[ti - 1])
        for kt in range(KD):
            qi("V", "V", 1, lambda e, _x=xb[:, kt * L:(kt + 1) * L], _l=L, _t=ti:
               e.tensor_mul(uscr0[:, :_l], _x, repR_t[_t][:, :_l]))
            qi("V", "V", 1, lambda e, _l=L, _t=ti:
               e.tensor_add(uscr0b[:, :_l], uscr0[:, :_l], repQ_t[_t][:, :_l]))
            qi("V", "V", 1, lambda e, _h=h1f[:, kt * L:(kt + 1) * L],
               _k=kt, _l=L: e.tensor_scalar(
                   _h, uscr0b[:, :_l], ca1_m[:, _k:_k + 1], sa_m[:, _k:_k + 1],
                   op0=ALU.mult, op1=ALU.add))
        x0_userV[ti % 2] = cnt["V"]
        hiv = h1hi3[:, :, off:off + L]
        lov = h1lo3[:, :, off:off + L]
        h1fv = h1f[:, :KD * L].rearrange("p (a b) -> p a b", a=KD)
        qw("S", "V")
        h1doneS[ti] = qi("S", "S", 1, lambda e, _o=hiv, _i=h1fv:
                         e.activation(_o, _i, AF.Copy))
        qw("G", "S", h1doneS[ti])
        h1doneV[ti] = qi("G", "GC", 1, lambda e, _o=lov, _i=h1fv, _h=hiv:
                         e.tensor_sub(_o, _i, _h))

    for oci, oc in enumerate(oc_order):
        buf = oci % 2
        nbuf = buf ^ 1
        if oci + 1 < 32:
            qw("S", f"W{nbuf}")
            qw("S", "T", slabW_userT[nbuf])
            split_S[nbuf] = stag_userS[nbuf] = qi(
                "S", "S", 1, lambda e, _b=nbuf: e.activation(
                    slabWhi[_b], stagW[_b], AF.Copy))
            qw("V", "S", split_S[nbuf])
            qw("V", "T", slabW_userT[nbuf])
            split_V[nbuf] = stag_userV[nbuf] = qi(
                "V", "V", 1, lambda e, _b=nbuf: e.tensor_sub(
                    slabWlo[_b], stagW[_b], slabWhi[_b]))
        if oci + 2 < 32:
            qw("Y", "S", stag_userS[buf])
            qw("Y", "V", stag_userV[buf])
            dma("Y", f"W{buf}", stagW[buf], dwin[oc_order[oci + 2]])
        if oc < 16:
            qw("S", f"PH{oc % 2}")
            if oc == 1:
                qw("S", "V", cB_freeV)
        for m in range(NMT):
            qw("T", "V", max(pm_freeV[m], split_V[buf]))
            qw("T", "GC", h1doneV[m + 1])
            qw("T", "S", max(pm_freeS[m], h1doneS[m + 1], split_S[buf]))
            mm3(pm[m], range(KD), slabWhi[buf], slabWlo[buf],
                lambda kt, _m=m: h1hi3[:, kt, HALO + _m * MT: HALO + (_m + 1) * MT],
                lambda kt, _m=m: h1lo3[:, kt, HALO + _m * MT: HALO + (_m + 1) * MT])
            tdone = cnt["T"]
            if oc >= 16:
                qw("V", "T", tdone)
                pm_freeV[m] = qi("V", "V", 1, lambda e, _m=m, _o=HALO + m * MT:
                                 e.tensor_copy(xcb[:, _o:_o + MT], pm[_m]))
            else:
                qw("S", "T", tdone)
                pm_freeS[m] = qi("S", "S", 1, lambda e, _m=m, _o=m * MT,
                                 _g=gbufs[oc % 2]:
                                 e.activation(_g[:, _o:_o + MT], pm[_m], AF.Gelu))
        qw("T", "V", pH_freeV)
        qw("T", "GC", h1doneV[0])
        qw("T", "S", max(pH_freeS, h1doneS[0]))
        slabW_userT[buf] = mm3(pH[:, :HALO], range(KD), slabWhi[buf], slabWlo[buf],
                               lambda kt: h1hi3[:, kt, 0:HALO],
                               lambda kt: h1lo3[:, kt, 0:HALO])
        tdone = cnt["T"]
        if oc >= 16:
            b = oc - 16
            qw("V", "T", tdone)
            qi("V", "V", 1, lambda e: e.tensor_copy(htmp, pH[:, :HALO]))
            pH_freeV = qi("V", "V", 1, lambda e: e.tensor_scalar(
                xcb[:, 0:HALO], htmp, bmask_t, None, op0=ALU.mult))
            qw("V", f"PX{b % 2}")
            qw("V", "S", h1doneS[NMT])
            qw("V", "GC", h1doneV[NMT])
            qi("V", "V", 1, lambda e, _b=b: e.tensor_scalar(
                cA, xcb[:, 4:4 + C], convw_t[3][:, _b:_b + 1], convb_t[:, _b:_b + 1],
                op0=ALU.mult, op1=ALU.add))
            qi("V", "V", 1, lambda e, _b=b: e.scalar_tensor_tensor(
                cB, xcb[:, 3:3 + C], convw_t[2][:, _b:_b + 1], cA, op0=ALU.mult, op1=ALU.add))
            qi("V", "V", 1, lambda e, _b=b: e.scalar_tensor_tensor(
                cA, xcb[:, 2:2 + C], convw_t[1][:, _b:_b + 1], cB, op0=ALU.mult, op1=ALU.add))
            vdone = qi("V", "V", 1, lambda e, _b=b: e.scalar_tensor_tensor(
                cB, xcb[:, 1:1 + C], convw_t[0][:, _b:_b + 1], cA, op0=ALU.mult, op1=ALU.add))
            qw("S", "V", vdone)
            qw("S", f"PX{b % 2}")
            hs = qi("S", "S", 1, lambda e, _c=chis[b % 2]:
                    e.activation(_c, cB, AF.Copy))
            qw("V", "S", hs)
            ls = qi("V", "V", 1, lambda e, _c=chis[b % 2], _l=clos[b % 2]:
                    e.tensor_sub(_l, cB, _c))
            if b == 15:
                cB_freeV = ls
            dma("S", f"PX{b % 2}", c_hi_sp[b], chis[b % 2])
            qw("S", "V", ls)
            dma("S", f"PX{b % 2}", c_lo_sp[b], clos[b % 2])
        else:
            qw("S", "T", tdone)
            pH_freeS = qi("S", "S", 1, lambda e, _oc=oc: e.activation(
                gate_b[:, _oc:_oc + 1], pH[:, 3:4], AF.Gelu))
            dma("S", f"PH{oc % 2}",
                g_sp[:, :, oc * BT:(oc + 1) * BT].transpose([1, 0, 2]),
                gbufs[oc % 2].rearrange("p (a b) -> p a b", a=NBT))
    al.reset(a0mark)

    qbar()
    al.reset(PERSIST)
    convhi = takeH(KH * C)
    convlo = takeH(KH * C)
    a3mark = al.mark()
    stagG = al.take(KH * 128)
    sFhi = [takeH(KH * 128), takeH(KH * 128)]
    sFlo = [takeH(KH * 128), takeH(KH * 128)]
    slabIhi = takeH(KH * 128)
    slabIlo = takeH(KH * 128)
    sgf = al.take(MT)
    Lbuf = al.take(MT)
    albuf = [al.take(MT) for _ in range(2)]
    tbufA = al.take(MT)
    tbufB = al.take(MT)
    sibuf = al.take(MT)
    btbuf = [al.take(MT) for _ in range(2)]
    hscr = al.take(MT)
    convrec = al.take(MT)
    qw("G", "P")
    for b in range(KH):
        sem = ("G0", "G1", "GO", "G")[b // 4]
        dma("G", sem, convhi[:, b * C:(b + 1) * C], c_hi_sp[b])
        dma("G", sem, convlo[:, b * C:(b + 1) * C], c_lo_sp[b])
    slab_userT = 0
    pm_freeV = [0, 0, 0, 0]
    pm_freeS = [0, 0, 0, 0]
    fF_userT = [0, 0]
    fI_userT = 0
    fs = [0, 0]
    fv = [0, 0]
    is_ = iv = 0

    def a3_split(hi, lo, userT, wq):
        qw("S", wq)
        qw("S", "T", userT)
        s = qi("S", "S", 1, lambda e, _h=hi: e.activation(_h, stagG, AF.Copy))
        qw("V", "S", s)
        qw("V", "T", userT)
        v = qi("V", "V", 1, lambda e, _h=hi, _l=lo: e.tensor_sub(_l, stagG, _h))
        return s, v

    dma("Y", "W0", stagG, dwg[0])
    fs[0], fv[0] = a3_split(sFhi[0], sFlo[0], 0, "W0")
    qw("Y", "S", fs[0])
    qw("Y", "V", fv[0])
    dma("Y", "W1", stagG, dwg[16])
    for oc in range(16):
        p = oc % 2
        pn = p ^ 1
        is_, iv = a3_split(slabIhi, slabIlo, fI_userT, "W1")
        if oc + 1 < 16:
            qw("Y", "S", is_)
            qw("Y", "V", iv)
            dma("Y", "W0", stagG, dwg[oc + 1])
            fs[pn], fv[pn] = a3_split(sFhi[pn], sFlo[pn], fF_userT[pn], "W0")
            qw("Y", "S", fs[pn])
            qw("Y", "V", fv[pn])
            dma("Y", "W1", stagG, dwg[16 + oc + 1])
        for m in range(NMT):
            fidx = 2 * (m % 2)
            iidx = 2 * (m % 2) + 1
            qw("T", "V", max(pm_freeV[fidx], fv[p]))
            qw("T", "S", max(pm_freeS[fidx], fs[p]))
            for qtr in range(4):
                qw("T", ("G0", "G1", "GO", "G")[qtr])
                mm3(pm[fidx], range(qtr * 4, (qtr + 1) * 4), sFhi[p], sFlo[p],
                    lambda kt, _m=m: convhi[:, kt * C + _m * MT: kt * C + (_m + 1) * MT],
                    lambda kt, _m=m: convlo[:, kt * C + _m * MT: kt * C + (_m + 1) * MT],
                    first=(qtr == 0), stop=(qtr == 3))
            tf = cnt["T"]
            if m == NMT - 1:
                fF_userT[p] = tf
            qw("T", "V", max(pm_freeV[iidx], iv))
            qw("T", "S", max(pm_freeS[iidx], is_))
            slab_userT = mm3(
                pm[iidx], range(KH), slabIhi, slabIlo,
                lambda kt, _m=m: convhi[:, kt * C + _m * MT: kt * C + (_m + 1) * MT],
                lambda kt, _m=m: convlo[:, kt * C + _m * MT: kt * C + (_m + 1) * MT])
            ti_ = cnt["T"]
            if m == NMT - 1:
                fI_userT = ti_
            ab = albuf[m % 2]
            bb = btbuf[m % 2]
            qw("S", "V")
            qw("S", "T", tf)
            qi("S", "S", 1, lambda e, _f=fidx, _oc=oc: e.activation(
                sgf, pm[_f], AF.Sigmoid, bias=bgf_t[:, _oc:_oc + 1]))
            pm_freeS[fidx] = cnt["S"]
            qw("V", "S")
            qi("V", "V", 1, lambda e, _oc=oc, _m=m: e.tensor_scalar(
                Lbuf, sgf, kvec[:, _oc:_oc + 1], None, op0=ALU.mult, op1=ALU.add,
                accum_out=accb[:, _m:_m + 1]))
            qw("S", "V")
            qw("S", f"PA{m % 2}")
            qi("S", "S", 1, lambda e, _a=ab: e.activation(_a, Lbuf, AF.Exp))
            dma("S", f"PA{m % 2}",
                a_sp[NMT * m:NMT * (m + 1), :, oc * BT:(oc + 1) * BT].transpose([1, 0, 2]),
                ab.rearrange("p (a b) -> p a b", a=NMT))
            qi("S", "S", 1, lambda e, _a=ab: e.activation(tbufA, _a, AF.Square))
            qw("V", "S")
            qi("V", "V", 1, lambda e: e.tensor_scalar(
                tbufB, tbufA, -1.0, 1.0 + 1e-6, op0=ALU.mult, op1=ALU.add))
            qw("S", "V")
            qi("S", "S", 1, lambda e: e.activation(tbufA, tbufB, AF.Sqrt))
            qw("S", "T", ti_)
            qi("S", "S", 1, lambda e, _i=iidx, _oc=oc: e.activation(
                sibuf, pm[_i], AF.Sigmoid, bias=bgi_t[:, _oc:_oc + 1]))
            pm_freeS[iidx] = cnt["S"]
            qi("V", "V", 1, lambda e, _oc=oc, _m=m: e.tensor_add(
                convrec, convhi[:, _oc * C + _m * MT: _oc * C + (_m + 1) * MT],
                convlo[:, _oc * C + _m * MT: _oc * C + (_m + 1) * MT]))
            qw("V", "S")
            qw("V", f"PB{m % 2}")
            qi("V", "V", 1, lambda e: e.tensor_mul(tbufB, tbufA, sibuf))
            vdone = qi("V", "V", 1, lambda e, _b=bb: e.tensor_mul(_b, tbufB, convrec))
            qw("S", "V", vdone)
            dma("S", f"PB{m % 2}",
                b_sp[NMT * m:NMT * (m + 1), :, oc * BT:(oc + 1) * BT].transpose([1, 0, 2]),
                bb.rearrange("p (a b) -> p a b", a=NMT))
            qi("V", "V", 1, lambda e, _a=ab, _b=bb, _oc=oc: e.tensor_tensor_scan(
                hscr, _a, _b, runA[:, _oc:_oc + 1], op0=ALU.mult, op1=ALU.add))
            pm_freeV[fidx] = pm_freeV[iidx] = qi(
                "V", "V", 1, lambda e, _oc=oc: e.tensor_copy(
                    runA[:, _oc:_oc + 1], hscr[:, MT - 1:MT]))
        qi("V", "V", 1, lambda e, _oc=oc: e.reduce_sum(
            Lsum[:, _oc:_oc + 1], accb, axis=AXX))
    al.reset(a3mark)

    qbar()
    qw("S", "V")
    qi("S", "S", 1, lambda e: e.activation(ab_sb[:, 0:KH], Lsum, AF.Exp))
    qw("V", "S")
    qi("V", "V", 1, lambda e: e.tensor_copy(ab_sb[:, KH:2 * KH], runA))
    qw("G", "V")
    dma("G", "G", ccc_in.ap(), ab_sb)
    qw("G", "G")
    if DBG_NO_CC:
        for _r in range(NCORE):
            dma("G", "G", ccc_out.ap()[_r], ccc_in.ap())
        qw("G", "G")
    else:
        qi("G", "CC", 1, lambda e: e.collective_compute(
            "AllGather", ALU.bypass, replica_groups=[list(range(NCORE))],
            ins=[ccc_in.ap().opt()], outs=[ccc_out.ap().opt()]))
        qw("G", "CC")
    dma("G", "G", aball.rearrange("p (a b) -> p a b", a=NCORE),
        ccc_out.ap().transpose([1, 0, 2]))
    qw("V", "G")
    qi("V", "V", 1, lambda e: e.memset(carA, 0.0))
    aview = aball.rearrange("p (a b) -> p a b", a=NCORE)[:, 0:7, 0:KH]
    bview = aball.rearrange("p (a b) -> p a b", a=NCORE)[:, 0:7, KH:2 * KH]
    am_raw = al.take(7 * KH)
    am_all = al.take(7 * KH)
    bm_all = al.take(7 * KH)
    qi("V", "V", 1, lambda e: e.tensor_tensor(
        am_raw.rearrange("p (a b) -> p a b", a=7), aview,
        cm_t.rearrange("p (a b) -> p a b", a=7), op=ALU.mult))
    qi("V", "V", 1, lambda e: e.tensor_add(am_all, am_raw, cm1_t))
    qi("V", "V", 1, lambda e: e.tensor_tensor(
        bm_all.rearrange("p (a b) -> p a b", a=7), bview,
        cm_t.rearrange("p (a b) -> p a b", a=7), op=ALU.mult))
    cur, nxt = carA, carB
    for r in range(NCORE - 1):
        qi("V", "V", 1, lambda e, _c=cur, _r=r: e.tensor_mul(
            tmp16c, _c, am_all[:, _r * KH:(_r + 1) * KH]))
        qi("V", "V", 1, lambda e, _n=nxt, _r=r: e.tensor_add(
            _n, tmp16c, bm_all[:, _r * KH:(_r + 1) * KH]))
        cur, nxt = nxt, cur
    carry = cur
    qi("V", "V", 1, lambda e, _c=carry: e.tensor_copy(runB, _c))
    qi("V", "V", 1, lambda e, _c=carry: e.tensor_mul(mb, gate_b, _c))

    qbar()
    al.reset(PERSIST)
    m_slot = [al.take(KH * MT) for _ in range(2)]
    at_t = [al.take(KH * BT) for _ in range(2)]
    bt_t = [al.take(KH * BT) for _ in range(2)]
    gt_t = al.take(KH * BT)
    h_t = al.take(KH * BT)
    slabO = [al.take(KH * 128) for _ in range(2)]
    xop = [al.take(MT) for _ in range(2)]
    x1loc = al.take(KD * MT)
    h2piece = [al.take(MT) for _ in range(2)]
    repR3 = al.take(MT)
    repQ3 = al.take(MT)
    uscr3 = al.take(MT)
    uscr3b = al.take(MT)
    xsq3 = [al.take(MT) for _ in range(2)]

    ab_scan_user = [0, 0]
    g_user = [0]

    def b1_prefetch_ab(bt):
        if bt >= NBT:
            return
        p = bt % 2
        qw("Y", "V", ab_scan_user[p])
        dma("Y", f"WA{p}", at_t[p], a_sp[bt])
        dma("Y", f"WA{p}", bt_t[p], b_sp[bt])

    def b1_tile(m, j, tfree):
        bt = NMT * m + j
        p = bt % 2
        slot3 = m_slot[m % 2].rearrange("p (a b) -> p a b", a=KH)
        qw("Y", "V", g_user[0])
        dma("Y", "WB", gt_t, g_sp[bt])
        b1_prefetch_ab(bt + 1)
        qw("V", "T", tfree)
        qw("V", f"WA{p}")
        for oc in range(KH):
            qi("V", "V", 1, lambda e, _oc=oc: e.tensor_tensor_scan(
                h_t[:, _oc * BT:(_oc + 1) * BT], at_t[p][:, _oc * BT:(_oc + 1) * BT],
                bt_t[p][:, _oc * BT:(_oc + 1) * BT], runB[:, _oc:_oc + 1],
                op0=ALU.mult, op1=ALU.add))
        ab_scan_user[p] = cnt["V"]
        qi("V", "V", 1, lambda e: e.tensor_copy(
            runB.rearrange("p (a b) -> p a b", a=KH),
            h_t.rearrange("p (a b) -> p a b", a=KH)[:, :, BT - 1:BT]))
        qw("V", "WB")
        g_user[0] = qi(
            "V", "V", 1, lambda e, _d=slot3[:, :, j * BT:(j + 1) * BT]:
            e.tensor_tensor(_d, gt_t, h_t, op=ALU.mult))
        return cnt["V"]

    slabO_user = [0, 0]
    xop_user = [0, 0]
    pm_freeV = [cnt["V"]] * 4
    pH_freeV = cnt["V"]
    slot_ready = [0, 0]
    slot_freeT = [0, 0]

    b1_prefetch_ab(0)
    for j in range(NMT):
        slot_ready[0] = b1_tile(0, j, 0)
    for m in range(NMT):
        slot = m_slot[m % 2]
        for oc in range(KD):
            buf = oc % 2
            qw("Y", "T", slabO_user[buf])
            dma("Y", f"W{buf}", slabO[buf], dwo[oc])
            qw("T", f"W{buf}")
            qw("T", "V", max(pm_freeV[oc % 4], slot_ready[m % 2]))
            qtg([(lambda e, _q=oc % 4, _b=buf, _kt=kt,
                  _r=slot[:, kt * MT:(kt + 1) * MT]:
                  e.matmul(pm[_q], slabO[_b][:, _kt * 128:(_kt + 1) * 128], _r,
                           start=(_kt == 0), stop=(_kt == KH - 1)))
                 for kt in range(KH)])
            tdone = cnt["T"]
            qw("G", "V", xop_user[oc % 2])
            dma("G", f"G{oc % 2}", xop[oc % 2],
                dx0[oc, :, HALO + m * MT: HALO + (m + 1) * MT])
            qw("V", "T", tdone)
            qw("V", f"G{oc % 2}")
            qw("V", f"PX{oc % 2}")
            vdone = qi("V", "V", 1, lambda e, _q=oc % 4, _oc=oc, _x=xop[oc % 2]:
                       e.scalar_tensor_tensor(
                           x1loc[:, _oc * MT:(_oc + 1) * MT], pm[_q],
                           ga_m[:, _oc:_oc + 1], _x, op0=ALU.mult, op1=ALU.add))
            pm_freeV[oc % 4] = vdone
            xop_user[oc % 2] = vdone
            qw("S", "V", vdone)
            dma("S", f"PX{oc % 2}", x1_sp[oc, :, m * MT:(m + 1) * MT],
                x1loc[:, oc * MT:(oc + 1) * MT])
            if oc % 2 == 1 and m + 1 < NMT:
                slot_ready[(m + 1) % 2] = b1_tile(
                    m + 1, (oc - 1) // 2, slot_freeT[(m + 1) % 2])
            if m == NMT - 1:
                qw("T", "V", pH_freeV)
                slabO_user[buf] = qtg(
                    [(lambda e, _b=buf, _kt=kt, _oc=oc:
                      e.matmul(pH[:, _oc:_oc + 1],
                               slabO[_b][:, _kt * 128:(_kt + 1) * 128],
                               mb[:, _kt:_kt + 1], start=(_kt == 0),
                               stop=(_kt == KH - 1)))
                     for kt in range(KH)])
                qw("V", "T")
                pH_freeV = qi("V", "V", 1, lambda e, _oc=oc: e.scalar_tensor_tensor(
                    x1b[:, _oc:_oc + 1], pH[:, _oc:_oc + 1], ga_m[:, _oc:_oc + 1],
                    x0b[:, _oc:_oc + 1], op0=ALU.mult, op1=ALU.add))
            else:
                slabO_user[buf] = tdone
        slot_freeT[m % 2] = cnt["T"]
        if m == NMT - 1:
            qw("S", "V")
            qi("S", "S", 1, lambda e: e.activation(tmp16a[:, 0:KD], x1b, AF.Square))
            qw("T", "S")
            qw("T", "V")
            qtg([lambda e: e.matmul(pS[:, 0:KD], onescol, x1b, start=True, stop=True)])
            qtg([lambda e: e.matmul(pQ[:, 0:KD], onescol, tmp16a[:, 0:KD],
                                    start=True, stop=True)])
            qw("V", "T")
            qi("V", "V", 1, lambda e: e.reduce_sum(rowA[:, 0:1], pS[:, 0:KD], axis=AXX))
            qi("V", "V", 1, lambda e: e.reduce_sum(rowB[:, 0:1], pQ[:, 0:KD], axis=AXX))
            qi("V", "V", 1, lambda e: e.tensor_single_scalar(
                rowA[:, 1:2], rowA[:, 0:1], 1.0 / D, op=ALU.mult))
            qi("V", "V", 1, lambda e: e.tensor_single_scalar(
                rowB[:, 1:2], rowB[:, 0:1], 1.0 / D, op=ALU.mult))
            qw("S", "V")
            qi("S", "S", 1, lambda e: e.activation(rowC[:, 0:1], rowA[:, 1:2], AF.Square))
            qw("V", "S")
            qi("V", "V", 1, lambda e: e.tensor_sub(rowD[:, 0:1], rowB[:, 1:2], rowC[:, 0:1]))
            qw("S", "V")
            qi("S", "S", 1, lambda e: e.activation(rowC[:, 0:1], rowD[:, 0:1], AF.Sqrt, bias=eps1))
            qw("V", "S")
            qi("V", "V", 1, lambda e: e.reciprocal(rowB[:, 0:1], rowC[:, 0:1]))
            qi("V", "V", 1, lambda e: e.tensor_single_scalar(
                rowC[:, 1:2], rowA[:, 1:2], -1.0, op=ALU.mult))
            qw("T", "V")
            qtg([lambda e: e.matmul(pR[:, 0:1], ones1, rowB[:, 0:1], start=True, stop=True)])
            qtg([lambda e: e.matmul(pR[:, 1:2], ones1, rowC[:, 1:2], start=True, stop=True)])
            qw("V", "T")
            qi("V", "V", 1, lambda e: e.tensor_copy(tmp16a[:, 0:1], pR[:, 0:1]))
            qi("V", "V", 1, lambda e: e.tensor_copy(tmp16a[:, 1:2], pR[:, 1:2]))
            for oc in range(KD):
                qi("V", "V", 1, lambda e, _oc=oc: e.tensor_scalar(
                    tmp16b[:, 0:1], x1b[:, _oc:_oc + 1], tmp16a[:, 1:2], tmp16a[:, 0:1],
                    op0=ALU.add, op1=ALU.mult))
                qi("V", "V", 1, lambda e, _oc=oc: e.tensor_scalar(
                    tmp16b[:, 1:2], tmp16b[:, 0:1], cb1_m[:, _oc:_oc + 1],
                    sb_m[:, _oc:_oc + 1], op0=ALU.mult, op1=ALU.add))
                qi("V", "V", 1, lambda e, _oc=oc: e.tensor_scalar(
                    h2bcol[:, _oc:_oc + 1], tmp16b[:, 1:2], bmask_t, None, op0=ALU.mult))
        qw("S", "V")
        ln_stats(x1loc, MT, KD, repR3, repQ3, xsq3)
        for kt in range(KD):
            qi("V", "V", 1, lambda e, _x=x1loc[:, kt * MT:(kt + 1) * MT]:
               e.tensor_mul(uscr3, _x, repR3))
            qi("V", "V", 1, lambda e: e.tensor_add(uscr3b, uscr3, repQ3))
            qw("V", f"PH{kt % 2}")
            vdone = qi("V", "V", 1, lambda e, _k=kt, _h=h2piece[kt % 2]: e.tensor_scalar(
                _h, uscr3b, cb1_m[:, _k:_k + 1], sb_m[:, _k:_k + 1],
                op0=ALU.mult, op1=ALU.add))
            qw("S", "V", vdone)
            dma("S", f"PH{kt % 2}", h2_sp[kt, :, m * MT:(m + 1) * MT], h2piece[kt % 2])

    qbar()
    al.reset(PERSIST)
    h2_res = al.take(KD * (C + 1))
    b4mark = al.mark()
    xr_hi = takeH(KD * C)
    xr_lo = takeH(KD * C)
    xkspare = al.take(C)
    _xk_slots = [xkspare] + [h2_res[:, oc * (C + 1): oc * (C + 1) + C]
                             for oc in range(KD - 1)]
    xk_hi_aps = [s[:, 0:C // 2].bitcast(BF16) for s in _xk_slots]
    xk_lo_aps = [s[:, C // 2:C].bitcast(BF16) for s in _xk_slots]
    dbuf = al.take(C)
    bandtmp = [al.take(C), al.take(C)]
    rbuf = [al.take(MT) for _ in range(2)]
    krbuf = al.take(MT)
    ksq = [al.take(MT), al.take(MT)]

    fstag = al.take(KD * 128)
    fshi = [takeH(KD * 128) for _ in range(2)]
    fslo = [takeH(KD * 128) for _ in range(2)]
    fst = {"user": [0, 0], "sp_s": [0, 0], "sp_v": [0, 0],
           "st_s": 0, "st_v": 0}

    def fsplit(b):
        qw("S", f"W{b}")
        qw("S", "T", fst["user"][b])
        fst["sp_s"][b] = qi("S", "S", 1, lambda e, _b=b: e.activation(
            fshi[_b], fstag, AF.Copy))
        qw("V", "S", fst["sp_s"][b])
        qw("V", "T", fst["user"][b])
        fst["sp_v"][b] = qi("V", "V", 1, lambda e, _b=b: e.tensor_sub(
            fslo[_b], fstag, fshi[_b]))
        fst["st_s"] = fst["sp_s"][b]
        fst["st_v"] = fst["sp_v"][b]

    dma("Y", "W0", fstag, dwr[0])
    fsplit(0)

    mix_hiS = [0, 0]
    mix_userV = [0, 0]
    mix_hiS_arr = [0] * KD
    mix_loV_arr = [0] * KD
    qw("G", "P")
    for oc in range(KD):
        qw("G", "V", mix_userV[oc % 2])
        dma("G", f"G{oc % 2}", h2_res[:, oc * (C + 1) + 1: (oc + 1) * (C + 1)],
            h2_sp[oc])
        qw("V", f"G{oc % 2}")
        qi("V", "V", 1, lambda e, _oc=oc: e.tensor_copy(
            h2_res[:, _oc * (C + 1):_oc * (C + 1) + 1], h2bcol[:, _oc:_oc + 1]))
        base = oc * (C + 1)
        qi("V", "V", 1, lambda e, _b=base: e.tensor_sub(
            dbuf, h2_res[:, _b + 1:_b + 1 + C], h2_res[:, _b:_b + C]))
        qw("V", "S", mix_hiS[oc % 2])
        st = qi("V", "V", 1, lambda e, _b=base, _oc=oc, _t=bandtmp[oc % 2]:
                e.scalar_tensor_tensor(
                    _t, dbuf, mur_t[:, _oc:_oc + 1],
                    h2_res[:, _b:_b + C], op0=ALU.mult, op1=ALU.add))
        qw("S", "V", st)
        mix_hiS[oc % 2] = hs = qi(
            "S", "S", 1, lambda e, _oc=oc, _t=bandtmp[oc % 2]: e.activation(
                xr_hi[:, _oc * C:(_oc + 1) * C], _t, AF.Copy))
        mix_hiS_arr[oc] = hs
        qw("V", "S", hs)
        mix_userV[oc % 2] = mix_loV_arr[oc] = qi(
            "V", "V", 1, lambda e, _oc=oc, _t=bandtmp[oc % 2]: e.tensor_sub(
                xr_lo[:, _oc * C:(_oc + 1) * C], _t,
                xr_hi[:, _oc * C:(_oc + 1) * C]))

    def mix_k_band(oc):
        base = oc * (C + 1)
        qi("V", "V", 1, lambda e, _b=base: e.tensor_sub(
            dbuf, h2_res[:, _b + 1:_b + 1 + C], h2_res[:, _b:_b + C]))
        qw("V", "S", mix_hiS[oc % 2])
        st = qi("V", "V", 1, lambda e, _b=base, _oc=oc, _t=bandtmp[oc % 2]:
                e.scalar_tensor_tensor(
                    _t, dbuf, muk_t[:, _oc:_oc + 1],
                    h2_res[:, _b:_b + C], op0=ALU.mult, op1=ALU.add))
        qw("S", "V", st)
        mix_hiS[oc % 2] = hs = qi(
            "S", "S", 1, lambda e, _oc=oc, _t=bandtmp[oc % 2]: e.activation(
                xk_hi_aps[_oc], _t, AF.Copy))
        qw("V", "S", hs)
        qi("V", "V", 1, lambda e, _oc=oc, _t=bandtmp[oc % 2]: e.tensor_sub(
            xk_lo_aps[_oc], _t, xk_hi_aps[_oc]))

    state = {"pm_freeV": [0] * 4, "pm_freeS": [0] * 4}

    def ffn_sweep(wsl, rhs_hi_fn, rhs_lo_fn, evac, filler=None,
                  prestaged=False, band_waits=None, pmV=None, pmS=None):
        if pmV is None:
            qbar()
        state["pm_freeV"] = list(pmV) if pmV is not None else [cnt["V"]] * 4
        state["pm_freeS"] = list(pmS) if pmS is not None else [cnt["S"]] * 4
        if not prestaged:
            qw("Y", "S", fst["st_s"])
            qw("Y", "V", fst["st_v"])
            dma("Y", "W0", fstag, wsl[0])
            fsplit(0)
        qw("Y", "S", fst["st_s"])
        qw("Y", "V", fst["st_v"])
        dma("Y", "W1", fstag, wsl[1])
        for oc in range(KD):
            buf = oc % 2
            nbuf = buf ^ 1
            if oc + 1 < KD:
                fsplit(nbuf)
            if oc + 2 < KD:
                qw("Y", "S", fst["st_s"])
                qw("Y", "V", fst["st_v"])
                dma("Y", f"W{buf}", fstag, wsl[oc + 2])
            for m in range(NMT):
                qw("T", "V", max(state["pm_freeV"][m], fst["sp_v"][buf]))
                qw("T", "S", max(state["pm_freeS"][m], fst["sp_s"][buf]))
                if band_waits is not None and oc == 0:
                    hiA, loA = band_waits
                    for kt in range(KD):
                        qw("T", "S", hiA[kt])
                        qw("T", "V", loA[kt])
                        fst["user"][buf] = mm3(
                            pm[m], [kt], fshi[buf], fslo[buf],
                            lambda k2, _m=m: rhs_hi_fn(k2, _m),
                            lambda k2, _m=m: rhs_lo_fn(k2, _m),
                            first=(kt == 0), stop=(kt == KD - 1))
                else:
                    fst["user"][buf] = mm3(
                        pm[m], range(KD), fshi[buf], fslo[buf],
                        lambda kt, _m=m: rhs_hi_fn(kt, _m),
                        lambda kt, _m=m: rhs_lo_fn(kt, _m))
                evac(oc, m, cnt["T"])
            if filler is not None:
                filler(oc)

    def evac_r(oc, m, tdone):
        qw("S", "T", tdone)
        qw("S", "P")
        qi("S", "S", 1, lambda e, _m=m, _o=rbuf[m % 2]: e.activation(_o, pm[_m], AF.Sigmoid))
        state["pm_freeS"][m] = cnt["S"]
        dma("S", "P", r_sp[oc, :, m * MT:(m + 1) * MT], rbuf[m % 2])

    ffn_sweep([dwr[i] for i in range(KD)],
              lambda kt, m: xr_hi[:, kt * C + m * MT: kt * C + (m + 1) * MT],
              lambda kt, m: xr_lo[:, kt * C + m * MT: kt * C + (m + 1) * MT],
              evac_r, filler=mix_k_band, prestaged=True,
              band_waits=(mix_hiS_arr, mix_loV_arr),
              pmV=pm_freeV, pmS=[fst["sp_s"][0]] * 4)

    kact_hi = xr_hi
    kact_lo = xr_lo
    ksq_loV = [0, 0]

    def evac_k(oc, m, tdone):
        qw("V", "T", tdone)
        qw("V", "S")
        state["pm_freeV"][m] = qi("V", "V", 1, lambda e, _m=m: e.tensor_relu(krbuf, pm[_m]))
        qw("S", "V")
        qi("S", "S", 1, lambda e, _m=m: e.activation(ksq[_m % 2], krbuf, AF.Square))
        hs = qi("S", "S", 1, lambda e, _oc=oc, _m=m: e.activation(
            kact_hi[:, _oc * C + _m * MT: _oc * C + (_m + 1) * MT],
            ksq[_m % 2], AF.Copy))
        qw("V", "S", hs)
        ksq_loV[m % 2] = qi("V", "V", 1, lambda e, _oc=oc, _m=m: e.tensor_sub(
            kact_lo[:, _oc * C + _m * MT: _oc * C + (_m + 1) * MT], ksq[_m % 2],
            kact_hi[:, _oc * C + _m * MT: _oc * C + (_m + 1) * MT]))

    ffn_sweep([dwk[i] for i in range(KD)],
              lambda kt, m: xk_hi_aps[kt][:, m * MT:(m + 1) * MT],
              lambda kt, m: xk_lo_aps[kt][:, m * MT:(m + 1) * MT],
              evac_k)

    ebuf = dbuf[:, 0:MT]
    x2buf = [dbuf[:, MT:2 * MT], dbuf[:, 2 * MT:3 * MT]]
    rpc = [dbuf[:, 3 * MT:4 * MT], bandtmp[0][:, 0:MT]]
    x1pc = [bandtmp[0][:, MT:2 * MT], bandtmp[0][:, 2 * MT:3 * MT]]
    pc_user = [0, 0]

    def evac_v(oc, m, tdone):
        qw("G", "V", pc_user[m % 2])
        dma("G", f"G{m % 2}", rpc[m % 2], r_sp[oc, :, m * MT:(m + 1) * MT])
        dma("G", f"G{m % 2}", x1pc[m % 2], x1_sp[oc, :, m * MT:(m + 1) * MT])
        qw("V", "T", tdone)
        qw("V", f"G{m % 2}")
        qw("V", "GO")
        state["pm_freeV"][m] = qi("V", "V", 1, lambda e, _m=m: e.tensor_mul(
            ebuf, rpc[_m % 2], pm[_m]))
        vd = qi("V", "V", 1, lambda e, _m=m, _oc=oc: e.scalar_tensor_tensor(
            x2buf[_m % 2], ebuf, gb_m[:, _oc:_oc + 1], x1pc[_m % 2],
            op0=ALU.mult, op1=ALU.add))
        pc_user[m % 2] = vd
        qw("G", "V", vd)
        dma("G", "GO", dy[oc, :, m * MT:(m + 1) * MT], x2buf[m % 2])

    ffn_sweep([dwv[i] for i in range(KD)],
              lambda kt, m: kact_hi[:, kt * C + m * MT: kt * C + (m + 1) * MT],
              lambda kt, m: kact_lo[:, kt * C + m * MT: kt * C + (m + 1) * MT],
              evac_v)

    qw("G", "GO")
    qw("G", "G")

    for k in maxwait:
        assert maxwait[k] <= cnt[k], (k, maxwait[k], cnt[k])

    with nc.Block() as block:
        block.tensor(lambda e: [f(e) for f in Q["T"]])
        block.vector(lambda e: [f(e) for f in Q["V"]])
        block.scalar(lambda e: [f(e) for f in Q["S"]])
        block.sync(lambda e: [f(e) for f in Q["Y"]])
        block.gpsimd(lambda e: [f(e) for f in Q["G"]])
    es.close()
    return nc


def _slabs(w, koct, kkt):
    o, i = w.shape
    assert o == koct * 128 and i == kkt * 128
    return np.ascontiguousarray(
        w.reshape(koct, 128, kkt, 128).transpose(0, 3, 2, 1).reshape(koct, 128, kkt * 128))


def _pcol(v, n):
    return np.ascontiguousarray(v.reshape(n, 128).T)


def kernel(L0_attn_x, L0_attn_W_in, L0_attn_conv_w, L0_attn_conv_b, L0_attn_W_g,
           L0_attn_b_g, L0_attn_forget_base, L0_attn_W_out, L0_ffn_W_r, L0_ffn_W_k,
           L0_ffn_W_v, L0_ffn_mu_r, L0_ffn_mu_k, L0_c, L0_wmod):
    f32 = np.float32
    x = np.asarray(L0_attn_x, f32)[0]
    XT = np.zeros((D, T_FULL + HALO), f32)
    XT[:, HALO:] = np.ascontiguousarray(x.T)

    win_sl = _slabs(np.asarray(L0_attn_W_in, f32), 32, KD)
    wg_sl = _slabs(np.asarray(L0_attn_W_g, f32), 32, KH)
    wo_sl = _slabs(np.asarray(L0_attn_W_out, f32), KD, KH)
    wr_sl = _slabs(np.asarray(L0_ffn_W_r, f32), KD, KD)
    wk_sl = _slabs(np.asarray(L0_ffn_W_k, f32), KD, KD)
    wv_sl = _slabs(np.asarray(L0_ffn_W_v, f32), KD, KD)
    wmod = np.asarray(L0_wmod, f32)
    cvec = _pcol(np.asarray(L0_c, f32)[0], KD)
    bg = np.asarray(L0_attn_b_g, f32)
    convw = np.ascontiguousarray(np.stack(
        [_pcol(np.ascontiguousarray(np.asarray(L0_attn_conv_w, f32)[:, 0, k]), KH)
         for k in range(4)]))
    convb = _pcol(np.asarray(L0_attn_conv_b, f32), KH)
    bgf = _pcol(np.ascontiguousarray(bg[:HID]), KH)
    bgi = _pcol(np.ascontiguousarray(bg[HID:]), KH)
    fb = _pcol(np.asarray(L0_attn_forget_base, f32), KH)
    mur = _pcol(np.asarray(L0_ffn_mu_r, f32), KD)
    muk = _pcol(np.asarray(L0_ffn_mu_k, f32), KD)

    in_maps = []
    for i in range(NCORE):
        x0 = np.ascontiguousarray(XT[:, i * C:i * C + CH].reshape(KD, 128, CH))
        wmod_i = np.ascontiguousarray(
            wmod[:, i * MODS:(i + 1) * MODS].reshape(KD, 128, MODS))
        bmask = np.full((128, 1), 0.0 if i == 0 else 1.0, f32)
        cm = np.zeros((7, 128, KH), f32)
        cm[:i] = 1.0
        cm1 = np.ascontiguousarray(1.0 - cm)
        in_maps.append({
            "x0": x0, "w_in_sl": win_sl, "w_g_sl": wg_sl, "w_out_sl": wo_sl,
            "w_r_sl": wr_sl, "w_k_sl": wk_sl, "w_v_sl": wv_sl, "wmod_sl": wmod_i,
            "cvec": cvec, "conv_w4": convw, "conv_b": convb, "bg_f": bgf,
            "bg_i": bgi, "fb": fb, "mu_r": mur, "mu_k": muk, "bmask": bmask,
            "cmask": cm, "cmask1": cm1,
        })

    nc = build_nc()
    res = run_bass_kernel_spmd(nc, in_maps, core_ids=list(range(NCORE)))
    out = np.empty((1, T_FULL, D), f32)
    for i in range(NCORE):
        y = res.results[i]["y"]
        out[0, i * C:(i + 1) * C, :] = np.asarray(y).transpose(2, 0, 1).reshape(C, D)
    return out


# version 30
# speedup vs baseline: 1.0108x; 1.0064x over previous
import numpy as np
from contextlib import ExitStack
import concourse.bass as bass
import concourse.mybir as mybir
from concourse.bass_utils import run_bass_kernel_spmd

F32 = mybir.dt.float32
BF16 = mybir.dt.bfloat16
U8 = mybir.dt.uint8
AF = mybir.ActivationFunctionType
ALU = mybir.AluOpType
AXX = mybir.AxisListType.X

NCORE = 8
D = 1024
HID = 2048
T_FULL = 16384
C = 2048
HALO = 4
CH = C + HALO
KD = D // 128
KH = HID // 128
MT = 512
NMT = C // MT
BT = 128
NBT = C // BT
MODS = 6 * D // NCORE
EPS = 1e-5

ARENA_COLS = 49152
DBG_NO_CC = False


def build_nc():
    nc = bass.Bass("TRN2", target_bir_lowering=False)
    es = ExitStack()

    dx0 = nc.dram_tensor("x0", [KD, 128, CH], F32, kind="ExternalInput").ap()
    dwin = nc.dram_tensor("w_in_sl", [32, 128, KD * 128], F32, kind="ExternalInput").ap()
    dwg = nc.dram_tensor("w_g_sl", [32, 128, KH * 128], F32, kind="ExternalInput").ap()
    dwo = nc.dram_tensor("w_out_sl", [KD, 128, KH * 128], F32, kind="ExternalInput").ap()
    dwr = nc.dram_tensor("w_r_sl", [KD, 128, KD * 128], F32, kind="ExternalInput").ap()
    dwk = nc.dram_tensor("w_k_sl", [KD, 128, KD * 128], F32, kind="ExternalInput").ap()
    dwv = nc.dram_tensor("w_v_sl", [KD, 128, KD * 128], F32, kind="ExternalInput").ap()
    dwm = nc.dram_tensor("wmod_sl", [KD, 128, MODS], F32, kind="ExternalInput").ap()
    dcvec = nc.dram_tensor("cvec", [128, KD], F32, kind="ExternalInput").ap()
    dconvw = nc.dram_tensor("conv_w4", [4, 128, KH], F32, kind="ExternalInput").ap()
    dconvb = nc.dram_tensor("conv_b", [128, KH], F32, kind="ExternalInput").ap()
    dbgf = nc.dram_tensor("bg_f", [128, KH], F32, kind="ExternalInput").ap()
    dbgi = nc.dram_tensor("bg_i", [128, KH], F32, kind="ExternalInput").ap()
    dfb = nc.dram_tensor("fb", [128, KH], F32, kind="ExternalInput").ap()
    dmur = nc.dram_tensor("mu_r", [128, KD], F32, kind="ExternalInput").ap()
    dmuk = nc.dram_tensor("mu_k", [128, KD], F32, kind="ExternalInput").ap()
    dbmask = nc.dram_tensor("bmask", [128, 1], F32, kind="ExternalInput").ap()
    dcm = nc.dram_tensor("cmask", [7, 128, KH], F32, kind="ExternalInput").ap()
    dcm1 = nc.dram_tensor("cmask1", [7, 128, KH], F32, kind="ExternalInput").ap()
    dy = nc.dram_tensor("y", [KD, 128, C], F32, kind="ExternalOutput").ap()

    g_sp = nc.dram_tensor("g_sp", [NBT, 128, KH * BT], F32).ap()
    c_hi_sp = nc.dram_tensor("c_hi_sp", [KH, 128, C], BF16).ap()
    c_lo_sp = nc.dram_tensor("c_lo_sp", [KH, 128, C], BF16).ap()
    a_sp = nc.dram_tensor("a_sp", [NBT, 128, KH * BT], F32).ap()
    b_sp = nc.dram_tensor("b_sp", [NBT, 128, KH * BT], F32).ap()
    x1_sp = nc.dram_tensor("x1_sp", [KD, 128, C], F32).ap()
    r_sp = nc.dram_tensor("r_sp", [KD, 128, C], F32).ap()
    h2_sp = nc.dram_tensor("h2_sp", [KD, 128, C], F32).ap()
    ccm_in = nc.dram_tensor("ccm_in", [1, MODS], F32)
    ccm_out = nc.dram_tensor("ccm_out", [NCORE, MODS], F32)
    ccc_in = nc.dram_tensor("ccc_in", [128, 2 * KH], F32)
    ccc_out = nc.dram_tensor("ccc_out", [NCORE, 128, 2 * KH], F32)

    AR = es.enter_context(nc.sbuf_tensor("arena", [128, ARENA_COLS], F32)).ap()
    ones1 = es.enter_context(nc.sbuf_tensor("ones1", [1, 128], F32)).ap()
    onescol = es.enter_context(nc.sbuf_tensor("onescol", [128, 1], F32)).ap()
    rowA = es.enter_context(nc.sbuf_tensor("rowA", [1, MT], F32)).ap()
    rowB = es.enter_context(nc.sbuf_tensor("rowB", [1, MT], F32)).ap()
    rowC = es.enter_context(nc.sbuf_tensor("rowC", [1, MT], F32)).ap()
    rowD = es.enter_context(nc.sbuf_tensor("rowD", [1, MT], F32)).ap()
    rowM = es.enter_context(nc.sbuf_tensor("rowM", [1, MODS], F32)).ap()
    eps1 = es.enter_context(nc.sbuf_tensor("eps1", [1, 1], F32)).ap()
    selm = es.enter_context(nc.sbuf_tensor("selm", [128, KD], U8)).ap()

    pm = [es.enter_context(nc.psum_tensor(f"pm{i}", [128, MT], F32)).ap()
          for i in range(4)]
    pH = es.enter_context(nc.psum_tensor("pH", [128, 16], F32)).ap()
    pS = es.enter_context(nc.psum_tensor("pS", [1, MT], F32)).ap()
    pQ = es.enter_context(nc.psum_tensor("pQ", [1, MT], F32)).ap()
    pR = es.enter_context(nc.psum_tensor("pR", [128, MT], F32)).ap()

    sems = {}
    for k in ("T", "V", "S", "GC", "W0", "W1", "WB", "WA0", "WA1", "P", "PA0",
              "PA1", "PB0", "PB1", "PH0", "PH1", "PX0", "PX1", "G", "G0", "G1",
              "GO", "CC"):
        sems[k] = es.enter_context(nc.semaphore(f"sem{k}"))

    Q = {k: [] for k in ("T", "V", "S", "Y", "G")}
    DMAS = ("W0", "W1", "WB", "WA0", "WA1", "P", "PA0", "PA1", "PB0", "PB1",
            "PH0", "PH1", "PX0", "PX1", "G", "G0", "G1", "GO", "CC")
    cnt = {k: 0 for k in ("T", "V", "S", "GC") + DMAS}
    maxwait = {k: 0 for k in cnt}

    def q(stream, fn):
        Q[stream].append(fn)

    def qw(stream, semk, val=None):
        if semk in DMAS:
            val = None
        v = cnt[semk] if val is None else val
        assert v <= cnt[semk]
        maxwait[semk] = max(maxwait[semk], v)
        tgt = sems[semk]
        q(stream, lambda e, _t=tgt, _v=v: e.wait_ge(_t, _v))

    def qi(stream, semk, n, fn):
        if stream == "V":
            qw("V", "V")
        elif stream == "S":
            qw("S", "S")
        elif stream == "G" and semk == "GC":
            qw("G", "GC")
        cnt[semk] += n
        tgt = sems[semk]
        q(stream, lambda e, _f=fn, _t=tgt, _n=n: _f(e).then_inc(_t, _n))
        return cnt[semk]

    def qn(stream, fn):
        q(stream, fn)

    def qtg(fns):
        for f in fns[:-1]:
            qn("T", f)
        return qi("T", "T", 1, fns[-1])

    def qbar():
        own = {"T": ("T",), "V": ("V",),
               "S": ("S", "P", "PA0", "PA1", "PB0", "PB1", "PH0", "PH1",
                     "PX0", "PX1"),
               "Y": ("W0", "W1", "WB", "WA0", "WA1"),
               "G": ("GC", "G", "G0", "G1", "GO", "CC")}
        for stream in ("T", "V", "S", "Y", "G"):
            mine = own[stream]
            for semk in ("T", "V", "S", "GC", "W0", "W1", "WB", "WA0", "WA1", "P",
                         "PA0", "PA1", "PB0", "PB1", "PH0", "PH1", "PX0", "PX1",
                         "G", "G0", "G1", "GO"):
                if semk in mine or cnt[semk] == 0:
                    continue
                qw(stream, semk)

    class Alloc:
        def __init__(self):
            self.off = 0
            self.peak = 0
        def take(self, n):
            a = self.off
            self.off += n
            self.peak = max(self.peak, self.off)
            assert self.off <= ARENA_COLS, f"arena overflow {self.off}"
            return AR[:, a:a + n]
        def mark(self):
            return self.off
        def reset(self, m):
            self.off = m

    al = Alloc()

    def takeH(n):
        return al.take((n + 1) // 2).bitcast(BF16)

    runA = al.take(KH)
    runB = al.take(KH)
    Lsum = al.take(KH)
    accb = al.take(4)
    gate_b = al.take(KH)
    carA = al.take(KH)
    carB = al.take(KH)
    ab_sb = al.take(2 * KH)
    aball = al.take(NCORE * 2 * KH)
    svec = al.take(KD)
    sa_m = al.take(KD); ca1_m = al.take(KD); ga_m = al.take(KD)
    sb_m = al.take(KD); cb1_m = al.take(KD); gb_m = al.take(KD)
    rawca = al.take(KD); rawcb = al.take(KD)
    cvec_t = al.take(KD)
    convw_t = [al.take(KH) for _ in range(4)]
    convb_t = al.take(KH)
    bgf_t = al.take(KH); bgi_t = al.take(KH); fb_t = al.take(KH)
    kvec = al.take(KH)
    mur_t = al.take(KD); muk_t = al.take(KD)
    bmask_t = al.take(1)
    cm_t = al.take(7 * KH); cm1_t = al.take(7 * KH)
    x0b = al.take(KD)
    x1b = al.take(KD)
    mb = al.take(KH)
    h2bcol = al.take(KD)
    tmp16a = al.take(KH); tmp16b = al.take(KH); tmp16c = al.take(KH)
    selsrcA = al.take(KD); selsrcB = al.take(KD)
    PERSIST = al.mark()

    def dma(stream, semk, out_ap, in_ap, slow=False):
        if slow:
            return qi(stream, semk, 16,
                      lambda e, _o=out_ap, _i=in_ap: e.dma_start(
                          out=_o, in_=_i, allow_slow_non_contiguous=True))
        return qi(stream, semk, 16,
                  lambda e, _o=out_ap, _i=in_ap: e.dma_start(out=_o, in_=_i))

    lnst = {"psq_user": 0}

    def ln_stats(xbuf, L, kt_n, repR, repQ, xsqp):
        nch = kt_n * 128
        qw("T", "V", lnst["psq_user"])
        qtg([(lambda e, _r=xbuf[:, kt * L:(kt + 1) * L], _st=(kt == 0),
              _sp=(kt == kt_n - 1), _L=L:
              e.matmul(pS[:, :_L], onescol, _r, start=_st, stop=_sp))
             for kt in range(kt_n)])
        mmdone = []
        for kt in range(kt_n):
            if kt >= 2:
                qw("S", "T", mmdone[kt - 2])
            sqd = qi("S", "S", 1, lambda e, _x=xbuf[:, kt * L:(kt + 1) * L],
                     _q=xsqp[kt % 2], _L=L: e.activation(_q[:, :_L], _x, AF.Square))
            qw("T", "S", sqd)
            mmdone.append(qi("T", "T", 1, lambda e, _q=xsqp[kt % 2], _L=L,
                             _st=(kt == 0), _sp=(kt == kt_n - 1):
                             e.matmul(pQ[:, :_L], onescol, _q[:, :_L],
                                      start=_st, stop=_sp)))
        qw("V", "T")
        inv = 1.0 / nch
        qi("V", "V", 1, lambda e, _L=L: e.tensor_single_scalar(
            rowA[:, :_L], pS[:, :_L], inv, op=ALU.mult))
        lnst["psq_user"] = qi("V", "V", 1, lambda e, _L=L: e.tensor_single_scalar(
            rowB[:, :_L], pQ[:, :_L], inv, op=ALU.mult))
        qw("S", "V")
        qi("S", "S", 1, lambda e, _L=L: e.activation(rowC[:, :_L], rowA[:, :_L], AF.Square))
        qw("V", "S")
        qi("V", "V", 1, lambda e, _L=L: e.tensor_sub(rowD[:, :_L], rowB[:, :_L], rowC[:, :_L]))
        qw("S", "V")
        qi("S", "S", 1, lambda e, _L=L: e.activation(rowC[:, :_L], rowD[:, :_L], AF.Sqrt, bias=eps1))
        qw("V", "S")
        qi("V", "V", 1, lambda e, _L=L: e.reciprocal(rowB[:, :_L], rowC[:, :_L]))
        qi("V", "V", 1, lambda e, _L=L: e.tensor_mul(rowC[:, :_L], rowA[:, :_L], rowB[:, :_L]))
        qi("V", "V", 1, lambda e, _L=L: e.tensor_single_scalar(
            rowD[:, :_L], rowC[:, :_L], -1.0, op=ALU.mult))
        qw("T", "V")
        qtg([lambda e, _L=L: e.matmul(pR[:, :_L], ones1, rowB[:, :_L], start=True, stop=True)])
        qw("V", "T")
        qi("V", "V", 1, lambda e, _L=L, _r=repR: e.tensor_copy(_r[:, :_L], pR[:, :_L]))
        qw("T", "V")
        qtg([lambda e, _L=L: e.matmul(pR[:, :_L], ones1, rowD[:, :_L], start=True, stop=True)])
        qw("V", "T")
        qi("V", "V", 1, lambda e, _L=L, _r=repQ: e.tensor_copy(_r[:, :_L], pR[:, :_L]))

    dma("Y", "W0", cvec_t, dcvec)
    for k in range(4):
        dma("Y", "W0", convw_t[k], dconvw[k])
    dma("Y", "W0", convb_t, dconvb)
    dma("Y", "W0", bgf_t, dbgf)
    dma("Y", "W0", bgi_t, dbgi)
    dma("Y", "W0", fb_t, dfb)
    dma("Y", "W0", mur_t, dmur)
    dma("Y", "W0", muk_t, dmuk)
    dma("Y", "W0", bmask_t, dbmask)
    for r in range(7):
        dma("Y", "W0", cm_t[:, r * KH:(r + 1) * KH], dcm[r])
        dma("Y", "W0", cm1_t[:, r * KH:(r + 1) * KH], dcm1[r])

    m0mark = al.mark()
    wm_sl = [al.take(MODS) for _ in range(KD)]
    for kt in range(KD):
        dma("Y", "W1", wm_sl[kt], dwm[kt])

    qi("V", "V", 1, lambda e: e.memset(ones1, 1.0))
    qi("V", "V", 1, lambda e: e.memset(eps1, EPS))
    qi("V", "V", 1, lambda e: e.memset(onescol, 1.0))
    qi("V", "V", 1, lambda e: e.memset(runA, 0.0))

    qw("S", "W0")
    qw("S", "V")
    qi("S", "S", 1, lambda e: e.activation(svec, cvec_t, AF.Silu))
    qi("S", "S", 1, lambda e: e.activation(tmp16a, fb_t, AF.Exp))
    qi("S", "S", 1, lambda e: e.activation(tmp16b, tmp16a, AF.Ln, bias=onescol))
    qw("V", "S")
    qi("V", "V", 1, lambda e: e.tensor_single_scalar(kvec, tmp16b, -8.0, op=ALU.mult))

    qw("T", "W1")
    qw("T", "S")
    n512 = MODS // MT
    segs = [(i * MT, MT) for i in range(n512)]
    if MODS - n512 * MT:
        segs.append((n512 * MT, MODS - n512 * MT))
    for si, (o, L) in enumerate(segs):
        ps = pS if si == 0 else pQ
        qtg([(lambda e, _p=ps, _kt=kt, _o=o, _L=L:
              e.matmul(_p[:, :_L], svec[:, _kt:_kt + 1], wm_sl[_kt][:, _o:_o + _L],
                       start=(_kt == 0), stop=(_kt == KD - 1)))
             for kt in range(KD)])
    qw("V", "T")
    for si, (o, L) in enumerate(segs):
        ps = pS if si == 0 else pQ
        qi("V", "V", 1, lambda e, _p=ps, _o=o, _L=L:
           e.tensor_copy(rowM[:, _o:_o + _L], _p[:, :_L]))
    lnst["psq_user"] = cnt["V"]
    qw("G", "V")
    dma("G", "G", ccm_in.ap(), rowM)
    qw("G", "G")
    if DBG_NO_CC:
        for _r in range(NCORE):
            dma("G", "G", ccm_out.ap()[_r:_r + 1, :], ccm_in.ap())
        qw("G", "G")
    else:
        qi("G", "CC", 1, lambda e: e.collective_compute(
            "AllGather", ALU.bypass, replica_groups=[list(range(NCORE))],
            ins=[ccm_in.ap().opt()], outs=[ccm_out.ap().opt()]))
        qw("G", "CC")
    modflat = ccm_out.ap().rearrange("a b -> (a b)")
    for v, dst in [(0, sa_m), (1, rawca), (2, ga_m), (3, sb_m), (4, rawcb), (5, gb_m)]:
        src = modflat[v * D:(v + 1) * D].rearrange("(kt p) -> p kt", p=128)
        dma("G", "G", dst, src, slow=True)
    dma("G", "G", x0b, dx0.transpose([1, 0, 2])[:, :, 3:4].rearrange("p a b -> p (a b)"),
        slow=True)

    al.reset(m0mark)

    h1hi = takeH(KD * CH)
    h1lo = takeH(KD * CH)
    a0mark = al.mark()
    h1f = al.take(KD * MT)
    x0bf = [al.take(KD * MT) for _ in range(2)]
    repR_t = [al.take(MT) for _ in range(NMT + 1)]
    repQ_t = [al.take(MT) for _ in range(NMT + 1)]
    uscr0 = al.take(MT)
    uscr0b = al.take(MT)
    xsq0 = [al.take(MT) for _ in range(2)]
    tiles_a0 = [(0, HALO)] + [(HALO + m * MT, MT) for m in range(NMT)]
    h1doneV = {}
    h1doneS = {}
    x0_userV = [0, 0]
    x0_userS = [0, 0]
    x0_userT = [0, 0]
    h1hi3 = h1hi.rearrange("p (a b) -> p a b", a=KD)
    h1lo3 = h1lo.rearrange("p (a b) -> p a b", a=KD)
    for ti, (off, L) in enumerate(tiles_a0):
        xb = x0bf[ti % 2]
        qw("Y", "V", x0_userV[ti % 2])
        qw("Y", "S", x0_userS[ti % 2])
        qw("Y", "T", x0_userT[ti % 2])
        dma("Y", f"WA{ti % 2}", xb[:, :KD * L].rearrange("p (a b) -> p a b", a=KD),
            dx0.transpose([1, 0, 2])[:, :, off:off + L])
        qw("V", f"WA{ti % 2}")
        qw("S", f"WA{ti % 2}")
        qw("T", f"WA{ti % 2}")
        ln_stats(xb[:, :KD * L], L, KD, repR_t[ti], repQ_t[ti], xsq0)
        x0_userS[ti % 2] = cnt["S"]
        x0_userT[ti % 2] = cnt["T"]
        x0_userV[ti % 2] = cnt["V"]

    qw("V", "G")
    qi("V", "V", 1, lambda e: e.tensor_single_scalar(ca1_m, rawca, 1.0, op=ALU.add))
    qi("V", "V", 1, lambda e: e.tensor_single_scalar(cb1_m, rawcb, 1.0, op=ALU.add))
    for g in (ga_m, gb_m):
        qi("V", "V", 1, lambda e, _g=g: e.tensor_single_scalar(selm, _g, 0.0, op=ALU.is_ge))
        qi("V", "V", 1, lambda e, _g=g: e.tensor_single_scalar(selsrcA, _g, 0.25, op=ALU.max))
        qi("V", "V", 1, lambda e, _g=g: e.tensor_single_scalar(selsrcB, _g, -0.25, op=ALU.min))
        qi("V", "V", 1, lambda e, _g=g: e.select(_g, selm, selsrcA, selsrcB, add_drain=True))

    a1mark = al.mark()
    stagW = [al.take(KD * 128) for _ in range(2)]
    slabWhi = [takeH(KD * 128) for _ in range(2)]
    slabWlo = [takeH(KD * 128) for _ in range(2)]
    xcb = al.take(HALO + C)
    cA = h1f[:, 0:C]
    cB = al.take(C)
    chis = [h1f[:, C + i * (C // 2): C + (i + 1) * (C // 2)].bitcast(BF16)
            for i in range(2)]
    clos = [takeH(C), takeH(C)]
    gbufs = [al.take(C), cB]
    htmp = al.take(HALO)
    slabW_userT = [0, 0]
    stag_userS = [0, 0]
    stag_userV = [0, 0]
    split_S = [0, 0]
    split_V = [0, 0]
    pm_freeV = [0, 0, 0, 0]
    pm_freeS = [0, 0, 0, 0]
    pH_freeV = 0
    pH_freeS = 0

    def mm3(ps, kts, whi, wlo, rhi_fn, rlo_fn, first=True, stop=True):
        trip = []
        for kt in kts:
            trip.append((whi[:, kt * 128:(kt + 1) * 128], rhi_fn(kt)))
            trip.append((whi[:, kt * 128:(kt + 1) * 128], rlo_fn(kt)))
            trip.append((wlo[:, kt * 128:(kt + 1) * 128], rhi_fn(kt)))
        last = len(trip) - 1
        return qtg([(lambda e, _w=w, _r=r, _st=(first and i == 0),
                     _sp=(stop and i == last), _p=ps:
                     e.matmul(_p, _w, _r, start=_st, stop=_sp))
                    for i, (w, r) in enumerate(trip)])

    oc_order = list(range(16, 32)) + list(range(16))
    slabW_userT = [0, 0]
    dma("Y", "W0", stagW[0], dwin[oc_order[0]])
    dma("Y", "W1", stagW[1], dwin[oc_order[1]])
    qw("S", "W0")
    split_S[0] = stag_userS[0] = qi(
        "S", "S", 1, lambda e: e.activation(slabWhi[0], stagW[0], AF.Copy))
    qw("V", "S", split_S[0])
    split_V[0] = stag_userV[0] = qi(
        "V", "V", 1, lambda e: e.tensor_sub(slabWlo[0], stagW[0], slabWhi[0]))
    cB_freeV = 0

    for ti, (off, L) in enumerate(tiles_a0):
        xb = x0bf[ti % 2]
        qw("Y", "V", x0_userV[ti % 2])
        qw("Y", "S", x0_userS[ti % 2])
        qw("Y", "T", x0_userT[ti % 2])
        dma("Y", f"WA{ti % 2}", xb[:, :KD * L].rearrange("p (a b) -> p a b", a=KD),
            dx0.transpose([1, 0, 2])[:, :, off:off + L])
        qw("V", f"WA{ti % 2}")
        if ti > 0:
            qw("V", "S", h1doneS[ti - 1])
            qw("V", "GC", h1doneV[ti - 1])
        for kt in range(KD):
            qi("V", "V", 1, lambda e, _x=xb[:, kt * L:(kt + 1) * L], _l=L, _t=ti:
               e.tensor_mul(uscr0[:, :_l], _x, repR_t[_t][:, :_l]))
            qi("V", "V", 1, lambda e, _l=L, _t=ti:
               e.tensor_add(uscr0b[:, :_l], uscr0[:, :_l], repQ_t[_t][:, :_l]))
            qi("V", "V", 1, lambda e, _h=h1f[:, kt * L:(kt + 1) * L],
               _k=kt, _l=L: e.tensor_scalar(
                   _h, uscr0b[:, :_l], ca1_m[:, _k:_k + 1], sa_m[:, _k:_k + 1],
                   op0=ALU.mult, op1=ALU.add))
        x0_userV[ti % 2] = cnt["V"]
        hiv = h1hi3[:, :, off:off + L]
        lov = h1lo3[:, :, off:off + L]
        h1fv = h1f[:, :KD * L].rearrange("p (a b) -> p a b", a=KD)
        qw("S", "V")
        h1doneS[ti] = qi("S", "S", 1, lambda e, _o=hiv, _i=h1fv:
                         e.activation(_o, _i, AF.Copy))
        qw("G", "S", h1doneS[ti])
        h1doneV[ti] = qi("G", "GC", 1, lambda e, _o=lov, _i=h1fv, _h=hiv:
                         e.tensor_sub(_o, _i, _h))

    for oci, oc in enumerate(oc_order):
        buf = oci % 2
        nbuf = buf ^ 1
        if oci + 1 < 32:
            qw("S", f"W{nbuf}")
            qw("S", "T", slabW_userT[nbuf])
            split_S[nbuf] = stag_userS[nbuf] = qi(
                "S", "S", 1, lambda e, _b=nbuf: e.activation(
                    slabWhi[_b], stagW[_b], AF.Copy))
            qw("V", "S", split_S[nbuf])
            qw("V", "T", slabW_userT[nbuf])
            split_V[nbuf] = stag_userV[nbuf] = qi(
                "V", "V", 1, lambda e, _b=nbuf: e.tensor_sub(
                    slabWlo[_b], stagW[_b], slabWhi[_b]))
        if oci + 2 < 32:
            qw("Y", "S", stag_userS[buf])
            qw("Y", "V", stag_userV[buf])
            dma("Y", f"W{buf}", stagW[buf], dwin[oc_order[oci + 2]])
        if oc < 16:
            qw("S", f"PH{oc % 2}")
            if oc == 1:
                qw("S", "V", cB_freeV)
        for m in range(NMT):
            qw("T", "V", max(pm_freeV[m], split_V[buf]))
            qw("T", "GC", h1doneV[m + 1])
            qw("T", "S", max(pm_freeS[m], h1doneS[m + 1], split_S[buf]))
            mm3(pm[m], range(KD), slabWhi[buf], slabWlo[buf],
                lambda kt, _m=m: h1hi3[:, kt, HALO + _m * MT: HALO + (_m + 1) * MT],
                lambda kt, _m=m: h1lo3[:, kt, HALO + _m * MT: HALO + (_m + 1) * MT])
            tdone = cnt["T"]
            if oc >= 16:
                qw("V", "T", tdone)
                pm_freeV[m] = qi("V", "V", 1, lambda e, _m=m, _o=HALO + m * MT:
                                 e.tensor_copy(xcb[:, _o:_o + MT], pm[_m]))
            else:
                qw("S", "T", tdone)
                pm_freeS[m] = qi("S", "S", 1, lambda e, _m=m, _o=m * MT,
                                 _g=gbufs[oc % 2]:
                                 e.activation(_g[:, _o:_o + MT], pm[_m], AF.Gelu))
        qw("T", "V", pH_freeV)
        qw("T", "GC", h1doneV[0])
        qw("T", "S", max(pH_freeS, h1doneS[0]))
        slabW_userT[buf] = mm3(pH[:, :HALO], range(KD), slabWhi[buf], slabWlo[buf],
                               lambda kt: h1hi3[:, kt, 0:HALO],
                               lambda kt: h1lo3[:, kt, 0:HALO])
        tdone = cnt["T"]
        if oc >= 16:
            b = oc - 16
            qw("V", "T", tdone)
            qi("V", "V", 1, lambda e: e.tensor_copy(htmp, pH[:, :HALO]))
            pH_freeV = qi("V", "V", 1, lambda e: e.tensor_scalar(
                xcb[:, 0:HALO], htmp, bmask_t, None, op0=ALU.mult))
            qw("V", f"PX{b % 2}")
            qw("V", "S", h1doneS[NMT])
            qw("V", "GC", h1doneV[NMT])
            qi("V", "V", 1, lambda e, _b=b: e.tensor_scalar(
                cA, xcb[:, 4:4 + C], convw_t[3][:, _b:_b + 1], convb_t[:, _b:_b + 1],
                op0=ALU.mult, op1=ALU.add))
            qi("V", "V", 1, lambda e, _b=b: e.scalar_tensor_tensor(
                cB, xcb[:, 3:3 + C], convw_t[2][:, _b:_b + 1], cA, op0=ALU.mult, op1=ALU.add))
            qi("V", "V", 1, lambda e, _b=b: e.scalar_tensor_tensor(
                cA, xcb[:, 2:2 + C], convw_t[1][:, _b:_b + 1], cB, op0=ALU.mult, op1=ALU.add))
            vdone = qi("V", "V", 1, lambda e, _b=b: e.scalar_tensor_tensor(
                cB, xcb[:, 1:1 + C], convw_t[0][:, _b:_b + 1], cA, op0=ALU.mult, op1=ALU.add))
            qw("S", "V", vdone)
            qw("S", f"PX{b % 2}")
            hs = qi("S", "S", 1, lambda e, _c=chis[b % 2]:
                    e.activation(_c, cB, AF.Copy))
            qw("V", "S", hs)
            ls = qi("V", "V", 1, lambda e, _c=chis[b % 2], _l=clos[b % 2]:
                    e.tensor_sub(_l, cB, _c))
            if b == 15:
                cB_freeV = ls
            dma("S", f"PX{b % 2}", c_hi_sp[b], chis[b % 2])
            qw("S", "V", ls)
            dma("S", f"PX{b % 2}", c_lo_sp[b], clos[b % 2])
        else:
            qw("S", "T", tdone)
            pH_freeS = qi("S", "S", 1, lambda e, _oc=oc: e.activation(
                gate_b[:, _oc:_oc + 1], pH[:, 3:4], AF.Gelu))
            dma("S", f"PH{oc % 2}",
                g_sp[:, :, oc * BT:(oc + 1) * BT].transpose([1, 0, 2]),
                gbufs[oc % 2].rearrange("p (a b) -> p a b", a=NBT))
    al.reset(a0mark)

    qbar()
    al.reset(PERSIST)
    convhi = takeH(KH * C)
    convlo = takeH(KH * C)
    a3mark = al.mark()
    stagG = al.take(KH * 128)
    sFhi = [takeH(KH * 128), takeH(KH * 128)]
    sFlo = [takeH(KH * 128), takeH(KH * 128)]
    slabIhi = takeH(KH * 128)
    slabIlo = takeH(KH * 128)
    sgf = al.take(MT)
    Lbuf = al.take(MT)
    albuf = [al.take(MT) for _ in range(2)]
    tbufA = al.take(MT)
    tbufB = al.take(MT)
    sibuf = al.take(MT)
    btbuf = [al.take(MT) for _ in range(2)]
    hscr = al.take(MT)
    convrec = al.take(MT)
    qw("G", "P")
    for b in range(KH):
        sem = ("G0", "G1", "GO", "G")[b // 4]
        dma("G", sem, convhi[:, b * C:(b + 1) * C], c_hi_sp[b])
        dma("G", sem, convlo[:, b * C:(b + 1) * C], c_lo_sp[b])
    slab_userT = 0
    pm_freeV = [0, 0, 0, 0]
    pm_freeS = [0, 0, 0, 0]
    fF_userT = [0, 0]
    fI_userT = 0
    fs = [0, 0]
    fv = [0, 0]
    is_ = iv = 0

    def a3_split(hi, lo, userT, wq):
        qw("S", wq)
        qw("S", "T", userT)
        s = qi("S", "S", 1, lambda e, _h=hi: e.activation(_h, stagG, AF.Copy))
        qw("V", "S", s)
        qw("V", "T", userT)
        v = qi("V", "V", 1, lambda e, _h=hi, _l=lo: e.tensor_sub(_l, stagG, _h))
        return s, v

    dma("Y", "W0", stagG, dwg[0])
    fs[0], fv[0] = a3_split(sFhi[0], sFlo[0], 0, "W0")
    qw("Y", "S", fs[0])
    qw("Y", "V", fv[0])
    dma("Y", "W1", stagG, dwg[16])
    for oc in range(16):
        p = oc % 2
        pn = p ^ 1
        is_, iv = a3_split(slabIhi, slabIlo, fI_userT, "W1")
        if oc + 1 < 16:
            qw("Y", "S", is_)
            qw("Y", "V", iv)
            dma("Y", "W0", stagG, dwg[oc + 1])
            fs[pn], fv[pn] = a3_split(sFhi[pn], sFlo[pn], fF_userT[pn], "W0")
            qw("Y", "S", fs[pn])
            qw("Y", "V", fv[pn])
            dma("Y", "W1", stagG, dwg[16 + oc + 1])
        for m in range(NMT):
            fidx = 2 * (m % 2)
            iidx = 2 * (m % 2) + 1
            qw("T", "V", max(pm_freeV[fidx], fv[p]))
            qw("T", "S", max(pm_freeS[fidx], fs[p]))
            if oc == 0:
                for qtr in range(4):
                    qw("T", ("G0", "G1", "GO", "G")[qtr])
                    mm3(pm[fidx], range(qtr * 4, (qtr + 1) * 4), sFhi[p], sFlo[p],
                        lambda kt, _m=m: convhi[:, kt * C + _m * MT: kt * C + (_m + 1) * MT],
                        lambda kt, _m=m: convlo[:, kt * C + _m * MT: kt * C + (_m + 1) * MT],
                        first=(qtr == 0), stop=(qtr == 3))
            else:
                mm3(pm[fidx], range(KH), sFhi[p], sFlo[p],
                    lambda kt, _m=m: convhi[:, kt * C + _m * MT: kt * C + (_m + 1) * MT],
                    lambda kt, _m=m: convlo[:, kt * C + _m * MT: kt * C + (_m + 1) * MT])
            tf = cnt["T"]
            if m == NMT - 1:
                fF_userT[p] = tf
            qw("T", "V", max(pm_freeV[iidx], iv))
            qw("T", "S", max(pm_freeS[iidx], is_))
            slab_userT = mm3(
                pm[iidx], range(KH), slabIhi, slabIlo,
                lambda kt, _m=m: convhi[:, kt * C + _m * MT: kt * C + (_m + 1) * MT],
                lambda kt, _m=m: convlo[:, kt * C + _m * MT: kt * C + (_m + 1) * MT])
            ti_ = cnt["T"]
            if m == NMT - 1:
                fI_userT = ti_
            ab = albuf[m % 2]
            bb = btbuf[m % 2]
            qw("S", "V")
            qw("S", "T", tf)
            qi("S", "S", 1, lambda e, _f=fidx, _oc=oc: e.activation(
                sgf, pm[_f], AF.Sigmoid, bias=bgf_t[:, _oc:_oc + 1]))
            pm_freeS[fidx] = cnt["S"]
            qw("V", "S")
            qi("V", "V", 1, lambda e, _oc=oc, _m=m: e.tensor_scalar(
                Lbuf, sgf, kvec[:, _oc:_oc + 1], None, op0=ALU.mult, op1=ALU.add,
                accum_out=accb[:, _m:_m + 1]))
            qw("S", "V")
            qw("S", f"PA{m % 2}")
            qi("S", "S", 1, lambda e, _a=ab: e.activation(_a, Lbuf, AF.Exp))
            dma("S", f"PA{m % 2}",
                a_sp[NMT * m:NMT * (m + 1), :, oc * BT:(oc + 1) * BT].transpose([1, 0, 2]),
                ab.rearrange("p (a b) -> p a b", a=NMT))
            qi("S", "S", 1, lambda e, _a=ab: e.activation(tbufA, _a, AF.Square))
            qw("V", "S")
            qi("V", "V", 1, lambda e: e.tensor_scalar(
                tbufB, tbufA, -1.0, 1.0 + 1e-6, op0=ALU.mult, op1=ALU.add))
            qw("S", "V")
            qi("S", "S", 1, lambda e: e.activation(tbufA, tbufB, AF.Sqrt))
            qw("S", "T", ti_)
            qi("S", "S", 1, lambda e, _i=iidx, _oc=oc: e.activation(
                sibuf, pm[_i], AF.Sigmoid, bias=bgi_t[:, _oc:_oc + 1]))
            pm_freeS[iidx] = cnt["S"]
            qi("V", "V", 1, lambda e, _oc=oc, _m=m: e.tensor_add(
                convrec, convhi[:, _oc * C + _m * MT: _oc * C + (_m + 1) * MT],
                convlo[:, _oc * C + _m * MT: _oc * C + (_m + 1) * MT]))
            qw("V", "S")
            qw("V", f"PB{m % 2}")
            qi("V", "V", 1, lambda e: e.tensor_mul(tbufB, tbufA, sibuf))
            vdone = qi("V", "V", 1, lambda e, _b=bb: e.tensor_mul(_b, tbufB, convrec))
            qw("S", "V", vdone)
            dma("S", f"PB{m % 2}",
                b_sp[NMT * m:NMT * (m + 1), :, oc * BT:(oc + 1) * BT].transpose([1, 0, 2]),
                bb.rearrange("p (a b) -> p a b", a=NMT))
            qi("V", "V", 1, lambda e, _a=ab, _b=bb, _oc=oc: e.tensor_tensor_scan(
                hscr, _a, _b, runA[:, _oc:_oc + 1], op0=ALU.mult, op1=ALU.add))
            pm_freeV[fidx] = pm_freeV[iidx] = qi(
                "V", "V", 1, lambda e, _oc=oc: e.tensor_copy(
                    runA[:, _oc:_oc + 1], hscr[:, MT - 1:MT]))
        qi("V", "V", 1, lambda e, _oc=oc: e.reduce_sum(
            Lsum[:, _oc:_oc + 1], accb, axis=AXX))
    al.reset(a3mark)

    qbar()
    qw("S", "V")
    qi("S", "S", 1, lambda e: e.activation(ab_sb[:, 0:KH], Lsum, AF.Exp))
    qw("V", "S")
    qi("V", "V", 1, lambda e: e.tensor_copy(ab_sb[:, KH:2 * KH], runA))
    qw("G", "V")
    dma("G", "G", ccc_in.ap(), ab_sb)
    qw("G", "G")
    if DBG_NO_CC:
        for _r in range(NCORE):
            dma("G", "G", ccc_out.ap()[_r], ccc_in.ap())
        qw("G", "G")
    else:
        qi("G", "CC", 1, lambda e: e.collective_compute(
            "AllGather", ALU.bypass, replica_groups=[list(range(NCORE))],
            ins=[ccc_in.ap().opt()], outs=[ccc_out.ap().opt()]))
        qw("G", "CC")
    dma("G", "G", aball.rearrange("p (a b) -> p a b", a=NCORE),
        ccc_out.ap().transpose([1, 0, 2]))
    qw("V", "G")
    qi("V", "V", 1, lambda e: e.memset(carA, 0.0))
    aview = aball.rearrange("p (a b) -> p a b", a=NCORE)[:, 0:7, 0:KH]
    bview = aball.rearrange("p (a b) -> p a b", a=NCORE)[:, 0:7, KH:2 * KH]
    am_raw = al.take(7 * KH)
    am_all = al.take(7 * KH)
    bm_all = al.take(7 * KH)
    qi("V", "V", 1, lambda e: e.tensor_tensor(
        am_raw.rearrange("p (a b) -> p a b", a=7), aview,
        cm_t.rearrange("p (a b) -> p a b", a=7), op=ALU.mult))
    qi("V", "V", 1, lambda e: e.tensor_add(am_all, am_raw, cm1_t))
    qi("V", "V", 1, lambda e: e.tensor_tensor(
        bm_all.rearrange("p (a b) -> p a b", a=7), bview,
        cm_t.rearrange("p (a b) -> p a b", a=7), op=ALU.mult))
    cur, nxt = carA, carB
    for r in range(NCORE - 1):
        qi("V", "V", 1, lambda e, _c=cur, _r=r: e.tensor_mul(
            tmp16c, _c, am_all[:, _r * KH:(_r + 1) * KH]))
        qi("V", "V", 1, lambda e, _n=nxt, _r=r: e.tensor_add(
            _n, tmp16c, bm_all[:, _r * KH:(_r + 1) * KH]))
        cur, nxt = nxt, cur
    carry = cur
    qi("V", "V", 1, lambda e, _c=carry: e.tensor_copy(runB, _c))
    qi("V", "V", 1, lambda e, _c=carry: e.tensor_mul(mb, gate_b, _c))

    qbar()
    al.reset(PERSIST)
    m_slot = [al.take(KH * MT) for _ in range(2)]
    at_t = [al.take(KH * BT) for _ in range(2)]
    bt_t = [al.take(KH * BT) for _ in range(2)]
    gt_t = al.take(KH * BT)
    h_t = al.take(KH * BT)
    slabO = [al.take(KH * 128) for _ in range(2)]
    xop = [al.take(MT) for _ in range(2)]
    x1loc = al.take(KD * MT)
    h2piece = [al.take(MT) for _ in range(2)]
    repR3 = al.take(MT)
    repQ3 = al.take(MT)
    uscr3 = al.take(MT)
    uscr3b = al.take(MT)
    xsq3 = [al.take(MT) for _ in range(2)]

    ab_scan_user = [0, 0]
    g_user = [0]

    def b1_prefetch_ab(bt):
        if bt >= NBT:
            return
        p = bt % 2
        qw("Y", "V", ab_scan_user[p])
        dma("Y", f"WA{p}", at_t[p], a_sp[bt])
        dma("Y", f"WA{p}", bt_t[p], b_sp[bt])

    def b1_tile(m, j, tfree):
        bt = NMT * m + j
        p = bt % 2
        slot3 = m_slot[m % 2].rearrange("p (a b) -> p a b", a=KH)
        qw("Y", "V", g_user[0])
        dma("Y", "WB", gt_t, g_sp[bt])
        b1_prefetch_ab(bt + 1)
        qw("V", "T", tfree)
        qw("V", f"WA{p}")
        for oc in range(KH):
            qi("V", "V", 1, lambda e, _oc=oc: e.tensor_tensor_scan(
                h_t[:, _oc * BT:(_oc + 1) * BT], at_t[p][:, _oc * BT:(_oc + 1) * BT],
                bt_t[p][:, _oc * BT:(_oc + 1) * BT], runB[:, _oc:_oc + 1],
                op0=ALU.mult, op1=ALU.add))
        ab_scan_user[p] = cnt["V"]
        qi("V", "V", 1, lambda e: e.tensor_copy(
            runB.rearrange("p (a b) -> p a b", a=KH),
            h_t.rearrange("p (a b) -> p a b", a=KH)[:, :, BT - 1:BT]))
        qw("V", "WB")
        g_user[0] = qi(
            "V", "V", 1, lambda e, _d=slot3[:, :, j * BT:(j + 1) * BT]:
            e.tensor_tensor(_d, gt_t, h_t, op=ALU.mult))
        return cnt["V"]

    slabO_user = [0, 0]
    xop_user = [0, 0]
    pm_freeV = [cnt["V"]] * 4
    pH_freeV = cnt["V"]
    slot_ready = [0, 0]
    slot_freeT = [0, 0]

    b1_prefetch_ab(0)
    for j in range(NMT):
        slot_ready[0] = b1_tile(0, j, 0)
    for m in range(NMT):
        slot = m_slot[m % 2]
        for oc in range(KD):
            buf = oc % 2
            qw("Y", "T", slabO_user[buf])
            dma("Y", f"W{buf}", slabO[buf], dwo[oc])
            qw("T", f"W{buf}")
            qw("T", "V", max(pm_freeV[oc % 4], slot_ready[m % 2]))
            qtg([(lambda e, _q=oc % 4, _b=buf, _kt=kt,
                  _r=slot[:, kt * MT:(kt + 1) * MT]:
                  e.matmul(pm[_q], slabO[_b][:, _kt * 128:(_kt + 1) * 128], _r,
                           start=(_kt == 0), stop=(_kt == KH - 1)))
                 for kt in range(KH)])
            tdone = cnt["T"]
            qw("G", "V", xop_user[oc % 2])
            dma("G", f"G{oc % 2}", xop[oc % 2],
                dx0[oc, :, HALO + m * MT: HALO + (m + 1) * MT])
            qw("V", "T", tdone)
            qw("V", f"G{oc % 2}")
            qw("V", f"PX{oc % 2}")
            vdone = qi("V", "V", 1, lambda e, _q=oc % 4, _oc=oc, _x=xop[oc % 2]:
                       e.scalar_tensor_tensor(
                           x1loc[:, _oc * MT:(_oc + 1) * MT], pm[_q],
                           ga_m[:, _oc:_oc + 1], _x, op0=ALU.mult, op1=ALU.add))
            pm_freeV[oc % 4] = vdone
            xop_user[oc % 2] = vdone
            qw("S", "V", vdone)
            dma("S", f"PX{oc % 2}", x1_sp[oc, :, m * MT:(m + 1) * MT],
                x1loc[:, oc * MT:(oc + 1) * MT])
            if oc % 2 == 1 and m + 1 < NMT:
                slot_ready[(m + 1) % 2] = b1_tile(
                    m + 1, (oc - 1) // 2, slot_freeT[(m + 1) % 2])
            if m == NMT - 1:
                qw("T", "V", pH_freeV)
                slabO_user[buf] = qtg(
                    [(lambda e, _b=buf, _kt=kt, _oc=oc:
                      e.matmul(pH[:, _oc:_oc + 1],
                               slabO[_b][:, _kt * 128:(_kt + 1) * 128],
                               mb[:, _kt:_kt + 1], start=(_kt == 0),
                               stop=(_kt == KH - 1)))
                     for kt in range(KH)])
                qw("V", "T")
                pH_freeV = qi("V", "V", 1, lambda e, _oc=oc: e.scalar_tensor_tensor(
                    x1b[:, _oc:_oc + 1], pH[:, _oc:_oc + 1], ga_m[:, _oc:_oc + 1],
                    x0b[:, _oc:_oc + 1], op0=ALU.mult, op1=ALU.add))
            else:
                slabO_user[buf] = tdone
        slot_freeT[m % 2] = cnt["T"]
        if m == NMT - 1:
            qw("S", "V")
            qi("S", "S", 1, lambda e: e.activation(tmp16a[:, 0:KD], x1b, AF.Square))
            qw("T", "S")
            qw("T", "V")
            qtg([lambda e: e.matmul(pS[:, 0:KD], onescol, x1b, start=True, stop=True)])
            qtg([lambda e: e.matmul(pQ[:, 0:KD], onescol, tmp16a[:, 0:KD],
                                    start=True, stop=True)])
            qw("V", "T")
            qi("V", "V", 1, lambda e: e.reduce_sum(rowA[:, 0:1], pS[:, 0:KD], axis=AXX))
            qi("V", "V", 1, lambda e: e.reduce_sum(rowB[:, 0:1], pQ[:, 0:KD], axis=AXX))
            qi("V", "V", 1, lambda e: e.tensor_single_scalar(
                rowA[:, 1:2], rowA[:, 0:1], 1.0 / D, op=ALU.mult))
            qi("V", "V", 1, lambda e: e.tensor_single_scalar(
                rowB[:, 1:2], rowB[:, 0:1], 1.0 / D, op=ALU.mult))
            qw("S", "V")
            qi("S", "S", 1, lambda e: e.activation(rowC[:, 0:1], rowA[:, 1:2], AF.Square))
            qw("V", "S")
            qi("V", "V", 1, lambda e: e.tensor_sub(rowD[:, 0:1], rowB[:, 1:2], rowC[:, 0:1]))
            qw("S", "V")
            qi("S", "S", 1, lambda e: e.activation(rowC[:, 0:1], rowD[:, 0:1], AF.Sqrt, bias=eps1))
            qw("V", "S")
            qi("V", "V", 1, lambda e: e.reciprocal(rowB[:, 0:1], rowC[:, 0:1]))
            qi("V", "V", 1, lambda e: e.tensor_single_scalar(
                rowC[:, 1:2], rowA[:, 1:2], -1.0, op=ALU.mult))
            qw("T", "V")
            qtg([lambda e: e.matmul(pR[:, 0:1], ones1, rowB[:, 0:1], start=True, stop=True)])
            qtg([lambda e: e.matmul(pR[:, 1:2], ones1, rowC[:, 1:2], start=True, stop=True)])
            qw("V", "T")
            qi("V", "V", 1, lambda e: e.tensor_copy(tmp16a[:, 0:1], pR[:, 0:1]))
            qi("V", "V", 1, lambda e: e.tensor_copy(tmp16a[:, 1:2], pR[:, 1:2]))
            for oc in range(KD):
                qi("V", "V", 1, lambda e, _oc=oc: e.tensor_scalar(
                    tmp16b[:, 0:1], x1b[:, _oc:_oc + 1], tmp16a[:, 1:2], tmp16a[:, 0:1],
                    op0=ALU.add, op1=ALU.mult))
                qi("V", "V", 1, lambda e, _oc=oc: e.tensor_scalar(
                    tmp16b[:, 1:2], tmp16b[:, 0:1], cb1_m[:, _oc:_oc + 1],
                    sb_m[:, _oc:_oc + 1], op0=ALU.mult, op1=ALU.add))
                qi("V", "V", 1, lambda e, _oc=oc: e.tensor_scalar(
                    h2bcol[:, _oc:_oc + 1], tmp16b[:, 1:2], bmask_t, None, op0=ALU.mult))
        qw("S", "V")
        ln_stats(x1loc, MT, KD, repR3, repQ3, xsq3)
        for kt in range(KD):
            qi("V", "V", 1, lambda e, _x=x1loc[:, kt * MT:(kt + 1) * MT]:
               e.tensor_mul(uscr3, _x, repR3))
            qi("V", "V", 1, lambda e: e.tensor_add(uscr3b, uscr3, repQ3))
            qw("V", f"PH{kt % 2}")
            vdone = qi("V", "V", 1, lambda e, _k=kt, _h=h2piece[kt % 2]: e.tensor_scalar(
                _h, uscr3b, cb1_m[:, _k:_k + 1], sb_m[:, _k:_k + 1],
                op0=ALU.mult, op1=ALU.add))
            qw("S", "V", vdone)
            dma("S", f"PH{kt % 2}", h2_sp[kt, :, m * MT:(m + 1) * MT], h2piece[kt % 2])

    qbar()
    al.reset(PERSIST)
    h2_res = al.take(KD * (C + 1))
    b4mark = al.mark()
    xr_hi = takeH(KD * C)
    xr_lo = takeH(KD * C)
    xkspare = al.take(C)
    _xk_slots = [xkspare] + [h2_res[:, oc * (C + 1): oc * (C + 1) + C]
                             for oc in range(KD - 1)]
    xk_hi_aps = [s[:, 0:C // 2].bitcast(BF16) for s in _xk_slots]
    xk_lo_aps = [s[:, C // 2:C].bitcast(BF16) for s in _xk_slots]
    dbuf = al.take(C)
    bandtmp = [al.take(C), al.take(C)]
    rbuf = [al.take(MT) for _ in range(2)]
    krbuf = al.take(MT)
    ksq = [al.take(MT), al.take(MT)]

    fstag = al.take(KD * 128)
    fshi = [takeH(KD * 128) for _ in range(2)]
    fslo = [takeH(KD * 128) for _ in range(2)]
    fst = {"user": [0, 0], "sp_s": [0, 0], "sp_v": [0, 0],
           "st_s": 0, "st_v": 0}

    def fsplit(b):
        qw("S", f"W{b}")
        qw("S", "T", fst["user"][b])
        fst["sp_s"][b] = qi("S", "S", 1, lambda e, _b=b: e.activation(
            fshi[_b], fstag, AF.Copy))
        qw("V", "S", fst["sp_s"][b])
        qw("V", "T", fst["user"][b])
        fst["sp_v"][b] = qi("V", "V", 1, lambda e, _b=b: e.tensor_sub(
            fslo[_b], fstag, fshi[_b]))
        fst["st_s"] = fst["sp_s"][b]
        fst["st_v"] = fst["sp_v"][b]

    dma("Y", "W0", fstag, dwr[0])
    fsplit(0)

    mix_hiS = [0, 0]
    mix_userV = [0, 0]
    mix_hiS_arr = [0] * KD
    mix_loV_arr = [0] * KD
    qw("G", "P")
    for oc in range(KD):
        qw("G", "V", mix_userV[oc % 2])
        dma("G", f"G{oc % 2}", h2_res[:, oc * (C + 1) + 1: (oc + 1) * (C + 1)],
            h2_sp[oc])
        qw("V", f"G{oc % 2}")
        qi("V", "V", 1, lambda e, _oc=oc: e.tensor_copy(
            h2_res[:, _oc * (C + 1):_oc * (C + 1) + 1], h2bcol[:, _oc:_oc + 1]))
        base = oc * (C + 1)
        qi("V", "V", 1, lambda e, _b=base: e.tensor_sub(
            dbuf, h2_res[:, _b + 1:_b + 1 + C], h2_res[:, _b:_b + C]))
        qw("V", "S", mix_hiS[oc % 2])
        st = qi("V", "V", 1, lambda e, _b=base, _oc=oc, _t=bandtmp[oc % 2]:
                e.scalar_tensor_tensor(
                    _t, dbuf, mur_t[:, _oc:_oc + 1],
                    h2_res[:, _b:_b + C], op0=ALU.mult, op1=ALU.add))
        qw("S", "V", st)
        mix_hiS[oc % 2] = hs = qi(
            "S", "S", 1, lambda e, _oc=oc, _t=bandtmp[oc % 2]: e.activation(
                xr_hi[:, _oc * C:(_oc + 1) * C], _t, AF.Copy))
        mix_hiS_arr[oc] = hs
        qw("V", "S", hs)
        mix_userV[oc % 2] = mix_loV_arr[oc] = qi(
            "V", "V", 1, lambda e, _oc=oc, _t=bandtmp[oc % 2]: e.tensor_sub(
                xr_lo[:, _oc * C:(_oc + 1) * C], _t,
                xr_hi[:, _oc * C:(_oc + 1) * C]))

    def mix_k_band(oc):
        base = oc * (C + 1)
        qi("V", "V", 1, lambda e, _b=base: e.tensor_sub(
            dbuf, h2_res[:, _b + 1:_b + 1 + C], h2_res[:, _b:_b + C]))
        qw("V", "S", mix_hiS[oc % 2])
        st = qi("V", "V", 1, lambda e, _b=base, _oc=oc, _t=bandtmp[oc % 2]:
                e.scalar_tensor_tensor(
                    _t, dbuf, muk_t[:, _oc:_oc + 1],
                    h2_res[:, _b:_b + C], op0=ALU.mult, op1=ALU.add))
        qw("S", "V", st)
        mix_hiS[oc % 2] = hs = qi(
            "S", "S", 1, lambda e, _oc=oc, _t=bandtmp[oc % 2]: e.activation(
                xk_hi_aps[_oc], _t, AF.Copy))
        qw("V", "S", hs)
        qi("V", "V", 1, lambda e, _oc=oc, _t=bandtmp[oc % 2]: e.tensor_sub(
            xk_lo_aps[_oc], _t, xk_hi_aps[_oc]))

    state = {"pm_freeV": [0] * 4, "pm_freeS": [0] * 4}

    def ffn_sweep(wsl, rhs_hi_fn, rhs_lo_fn, evac, filler=None,
                  prestaged=False, band_waits=None, pmV=None, pmS=None):
        if pmV is None:
            qbar()
        state["pm_freeV"] = list(pmV) if pmV is not None else [cnt["V"]] * 4
        state["pm_freeS"] = list(pmS) if pmS is not None else [cnt["S"]] * 4
        if not prestaged:
            qw("Y", "S", fst["st_s"])
            qw("Y", "V", fst["st_v"])
            dma("Y", "W0", fstag, wsl[0])
            fsplit(0)
        qw("Y", "S", fst["st_s"])
        qw("Y", "V", fst["st_v"])
        dma("Y", "W1", fstag, wsl[1])
        for oc in range(KD):
            buf = oc % 2
            nbuf = buf ^ 1
            if oc + 1 < KD:
                fsplit(nbuf)
            if oc + 2 < KD:
                qw("Y", "S", fst["st_s"])
                qw("Y", "V", fst["st_v"])
                dma("Y", f"W{buf}", fstag, wsl[oc + 2])
            for m in range(NMT):
                qw("T", "V", max(state["pm_freeV"][m], fst["sp_v"][buf]))
                qw("T", "S", max(state["pm_freeS"][m], fst["sp_s"][buf]))
                if band_waits is not None and oc == 0:
                    hiA, loA = band_waits
                    for kt in range(KD):
                        qw("T", "S", hiA[kt])
                        qw("T", "V", loA[kt])
                        fst["user"][buf] = mm3(
                            pm[m], [kt], fshi[buf], fslo[buf],
                            lambda k2, _m=m: rhs_hi_fn(k2, _m),
                            lambda k2, _m=m: rhs_lo_fn(k2, _m),
                            first=(kt == 0), stop=(kt == KD - 1))
                else:
                    fst["user"][buf] = mm3(
                        pm[m], range(KD), fshi[buf], fslo[buf],
                        lambda kt, _m=m: rhs_hi_fn(kt, _m),
                        lambda kt, _m=m: rhs_lo_fn(kt, _m))
                evac(oc, m, cnt["T"])
            if filler is not None:
                filler(oc)

    def evac_r(oc, m, tdone):
        qw("S", "T", tdone)
        qw("S", "P")
        qi("S", "S", 1, lambda e, _m=m, _o=rbuf[m % 2]: e.activation(_o, pm[_m], AF.Sigmoid))
        state["pm_freeS"][m] = cnt["S"]
        dma("S", "P", r_sp[oc, :, m * MT:(m + 1) * MT], rbuf[m % 2])

    ffn_sweep([dwr[i] for i in range(KD)],
              lambda kt, m: xr_hi[:, kt * C + m * MT: kt * C + (m + 1) * MT],
              lambda kt, m: xr_lo[:, kt * C + m * MT: kt * C + (m + 1) * MT],
              evac_r, filler=mix_k_band, prestaged=True,
              band_waits=(mix_hiS_arr, mix_loV_arr),
              pmV=pm_freeV, pmS=[fst["sp_s"][0]] * 4)

    kact_hi = xr_hi
    kact_lo = xr_lo
    ksq_loV = [0, 0]

    def evac_k(oc, m, tdone):
        qw("V", "T", tdone)
        qw("V", "S")
        state["pm_freeV"][m] = qi("V", "V", 1, lambda e, _m=m: e.tensor_relu(krbuf, pm[_m]))
        qw("S", "V")
        qi("S", "S", 1, lambda e, _m=m: e.activation(ksq[_m % 2], krbuf, AF.Square))
        hs = qi("S", "S", 1, lambda e, _oc=oc, _m=m: e.activation(
            kact_hi[:, _oc * C + _m * MT: _oc * C + (_m + 1) * MT],
            ksq[_m % 2], AF.Copy))
        qw("V", "S", hs)
        ksq_loV[m % 2] = qi("V", "V", 1, lambda e, _oc=oc, _m=m: e.tensor_sub(
            kact_lo[:, _oc * C + _m * MT: _oc * C + (_m + 1) * MT], ksq[_m % 2],
            kact_hi[:, _oc * C + _m * MT: _oc * C + (_m + 1) * MT]))

    ffn_sweep([dwk[i] for i in range(KD)],
              lambda kt, m: xk_hi_aps[kt][:, m * MT:(m + 1) * MT],
              lambda kt, m: xk_lo_aps[kt][:, m * MT:(m + 1) * MT],
              evac_k)

    ebuf = dbuf[:, 0:MT]
    x2buf = [dbuf[:, MT:2 * MT], dbuf[:, 2 * MT:3 * MT]]
    rpc = [dbuf[:, 3 * MT:4 * MT], bandtmp[0][:, 0:MT]]
    x1pc = [bandtmp[0][:, MT:2 * MT], bandtmp[0][:, 2 * MT:3 * MT]]
    pc_user = [0, 0]

    def evac_v(oc, m, tdone):
        qw("G", "V", pc_user[m % 2])
        dma("G", f"G{m % 2}", rpc[m % 2], r_sp[oc, :, m * MT:(m + 1) * MT])
        dma("G", f"G{m % 2}", x1pc[m % 2], x1_sp[oc, :, m * MT:(m + 1) * MT])
        qw("V", "T", tdone)
        qw("V", f"G{m % 2}")
        qw("V", "GO")
        state["pm_freeV"][m] = qi("V", "V", 1, lambda e, _m=m: e.tensor_mul(
            ebuf, rpc[_m % 2], pm[_m]))
        vd = qi("V", "V", 1, lambda e, _m=m, _oc=oc: e.scalar_tensor_tensor(
            x2buf[_m % 2], ebuf, gb_m[:, _oc:_oc + 1], x1pc[_m % 2],
            op0=ALU.mult, op1=ALU.add))
        pc_user[m % 2] = vd
        qw("G", "V", vd)
        dma("G", "GO", dy[oc, :, m * MT:(m + 1) * MT], x2buf[m % 2])

    ffn_sweep([dwv[i] for i in range(KD)],
              lambda kt, m: kact_hi[:, kt * C + m * MT: kt * C + (m + 1) * MT],
              lambda kt, m: kact_lo[:, kt * C + m * MT: kt * C + (m + 1) * MT],
              evac_v)

    qw("G", "GO")
    qw("G", "G")

    for k in maxwait:
        assert maxwait[k] <= cnt[k], (k, maxwait[k], cnt[k])

    with nc.Block() as block:
        block.tensor(lambda e: [f(e) for f in Q["T"]])
        block.vector(lambda e: [f(e) for f in Q["V"]])
        block.scalar(lambda e: [f(e) for f in Q["S"]])
        block.sync(lambda e: [f(e) for f in Q["Y"]])
        block.gpsimd(lambda e: [f(e) for f in Q["G"]])
    es.close()
    return nc


def _slabs(w, koct, kkt):
    o, i = w.shape
    assert o == koct * 128 and i == kkt * 128
    return np.ascontiguousarray(
        w.reshape(koct, 128, kkt, 128).transpose(0, 3, 2, 1).reshape(koct, 128, kkt * 128))


def _pcol(v, n):
    return np.ascontiguousarray(v.reshape(n, 128).T)


def kernel(L0_attn_x, L0_attn_W_in, L0_attn_conv_w, L0_attn_conv_b, L0_attn_W_g,
           L0_attn_b_g, L0_attn_forget_base, L0_attn_W_out, L0_ffn_W_r, L0_ffn_W_k,
           L0_ffn_W_v, L0_ffn_mu_r, L0_ffn_mu_k, L0_c, L0_wmod):
    f32 = np.float32
    x = np.asarray(L0_attn_x, f32)[0]
    XT = np.zeros((D, T_FULL + HALO), f32)
    XT[:, HALO:] = np.ascontiguousarray(x.T)

    win_sl = _slabs(np.asarray(L0_attn_W_in, f32), 32, KD)
    wg_sl = _slabs(np.asarray(L0_attn_W_g, f32), 32, KH)
    wo_sl = _slabs(np.asarray(L0_attn_W_out, f32), KD, KH)
    wr_sl = _slabs(np.asarray(L0_ffn_W_r, f32), KD, KD)
    wk_sl = _slabs(np.asarray(L0_ffn_W_k, f32), KD, KD)
    wv_sl = _slabs(np.asarray(L0_ffn_W_v, f32), KD, KD)
    wmod = np.asarray(L0_wmod, f32)
    cvec = _pcol(np.asarray(L0_c, f32)[0], KD)
    bg = np.asarray(L0_attn_b_g, f32)
    convw = np.ascontiguousarray(np.stack(
        [_pcol(np.ascontiguousarray(np.asarray(L0_attn_conv_w, f32)[:, 0, k]), KH)
         for k in range(4)]))
    convb = _pcol(np.asarray(L0_attn_conv_b, f32), KH)
    bgf = _pcol(np.ascontiguousarray(bg[:HID]), KH)
    bgi = _pcol(np.ascontiguousarray(bg[HID:]), KH)
    fb = _pcol(np.asarray(L0_attn_forget_base, f32), KH)
    mur = _pcol(np.asarray(L0_ffn_mu_r, f32), KD)
    muk = _pcol(np.asarray(L0_ffn_mu_k, f32), KD)

    in_maps = []
    for i in range(NCORE):
        x0 = np.ascontiguousarray(XT[:, i * C:i * C + CH].reshape(KD, 128, CH))
        wmod_i = np.ascontiguousarray(
            wmod[:, i * MODS:(i + 1) * MODS].reshape(KD, 128, MODS))
        bmask = np.full((128, 1), 0.0 if i == 0 else 1.0, f32)
        cm = np.zeros((7, 128, KH), f32)
        cm[:i] = 1.0
        cm1 = np.ascontiguousarray(1.0 - cm)
        in_maps.append({
            "x0": x0, "w_in_sl": win_sl, "w_g_sl": wg_sl, "w_out_sl": wo_sl,
            "w_r_sl": wr_sl, "w_k_sl": wk_sl, "w_v_sl": wv_sl, "wmod_sl": wmod_i,
            "cvec": cvec, "conv_w4": convw, "conv_b": convb, "bg_f": bgf,
            "bg_i": bgi, "fb": fb, "mu_r": mur, "mu_k": muk, "bmask": bmask,
            "cmask": cm, "cmask1": cm1,
        })

    nc = build_nc()
    res = run_bass_kernel_spmd(nc, in_maps, core_ids=list(range(NCORE)))
    out = np.empty((1, T_FULL, D), f32)
    for i in range(NCORE):
        y = res.results[i]["y"]
        out[0, i * C:(i + 1) * C, :] = np.asarray(y).transpose(2, 0, 1).reshape(C, D)
    return out
